# version 07
# baseline (speedup 1.0000x reference)
import numpy as np
from contextlib import ExitStack

import concourse.bass as bass
import concourse.mybir as mybir
from concourse.bass_utils import run_bass_kernel_spmd

F32 = mybir.dt.float32
BF16 = mybir.dt.bfloat16
AF = mybir.ActivationFunctionType
ALU = mybir.AluOpType
AX = mybir.AxisListType

T = 4096
DM = 1024
DI = 2048
NS = 16
RK = 64
HI = 4096
C = 512
NCH = T // C
MT = C // 128
KC = DM // 128
DC = DI // 128
HC = HI // 128

ENGS = ("sync", "tensor", "vector", "scalar", "gpsimd")


DMA_POOLS = {"x": 8, "w": 12, "bc": 3, "o": 4}


class Prog:

    def __init__(self, emit_eng, eng_obj, sems):
        self.emit = emit_eng
        self.e = eng_obj
        self.sems = sems
        self.ctr = {k: 0 for k in sems}
        self.rot = {k: 0 for k in DMA_POOLS}
        self.lk_budget = 0
        self.lk_skip = 0

    def op(self, eng, fn, sem=None, inc=1):
        if self.lk_skip > 0:
            self.lk_skip -= 1
            lk = False
        else:
            lk = self.lk_budget > 0
        if lk:
            self.lk_budget -= 1
            if sem is None:
                sem = f"lk_{eng}"
        if eng == self.emit:
            ins = fn(self.e)
            if sem is not None:
                ins.then_inc(self.sems[sem], inc)
        if sem is not None:
            self.ctr[sem] += inc
            if lk:
                self.wait(self.emit, sem, self.ctr[sem])

    def dma(self, eng, out, in_, cls):
        k = self.rot[cls] % DMA_POOLS[cls]
        self.rot[cls] += 1
        sem = f"{cls}{k}"
        if self.ctr[sem] > 0:
            self.wait(eng, sem, self.ctr[sem])
        lk = self.lk_skip <= 0 and self.lk_budget > 0
        b, self.lk_budget = self.lk_budget, 0
        self.op(eng, lambda e: e.dma_start(out=out, in_=in_), sem=sem, inc=16)
        self.lk_budget = (b - 1) if lk else b
        if lk:
            self.wait(self.emit, sem, self.ctr[sem])
        return (sem, self.ctr[sem])

    def wait_t(self, eng, ticket):
        if ticket is None:
            return
        sem, val = ticket
        self.wait(eng, sem, val)

    def mark_all(self, cls):
        return tuple((f"{cls}{k}", self.ctr[f"{cls}{k}"])
                     for k in range(DMA_POOLS[cls]))

    def wait_all(self, eng, marks):
        for sem, val in marks:
            self.wait(eng, sem, val)

    def wait(self, eng, sem, val=None):
        if val is None:
            val = self.ctr[sem]
        assert val <= self.ctr[sem], ("wait on future inc", sem, val, self.ctr[sem])
        if val <= 0:
            return
        if eng == self.emit:
            self.e.wait_ge(self.sems[sem], val)

    def mark(self, sem):
        return self.ctr[sem]


def build_nc(report=None, stop_after=None, serial=False, mode="full",
             recorder=None, dbg=False, dbg2=False, lockstep=0,
             lockstep_skip=0):
    nc = bass.Bass("TRN2", target_bir_lowering=False, debug=False)

    x_d = nc.dram_tensor("x", [T, DM], F32, kind="ExternalInput").ap()
    Win_d = nc.dram_tensor("Win", [DM, 2 * DI], F32, kind="ExternalInput").ap()
    bin_d = nc.dram_tensor("bin", [2 * DI], F32, kind="ExternalInput").ap()
    Wx_d = nc.dram_tensor("Wx", [DI, 96], F32, kind="ExternalInput").ap()
    bx_d = nc.dram_tensor("bx", [96], F32, kind="ExternalInput").ap()
    Wdt_d = nc.dram_tensor("Wdt", [RK, DI], F32, kind="ExternalInput").ap()
    bdt_d = nc.dram_tensor("bdt", [DI], F32, kind="ExternalInput").ap()
    Alog_d = nc.dram_tensor("A_log", [DI, NS], F32, kind="ExternalInput").ap()
    Dp_d = nc.dram_tensor("Dp", [DI], F32, kind="ExternalInput").ap()
    Wout_d = nc.dram_tensor("Wout", [DI, DM], F32, kind="ExternalInput").ap()
    bout_d = nc.dram_tensor("bout", [DM], F32, kind="ExternalInput").ap()
    W1_d = nc.dram_tensor("W1", [DM, HI], F32, kind="ExternalInput").ap()
    b1_d = nc.dram_tensor("b1", [HI], F32, kind="ExternalInput").ap()
    W2_d = nc.dram_tensor("W2", [HI, DM], F32, kind="ExternalInput").ap()
    b2_d = nc.dram_tensor("b2", [DM], F32, kind="ExternalInput").ap()
    out_d = nc.dram_tensor("out", [T, DM], F32, kind="ExternalOutput").ap()
    dbg_d = (nc.dram_tensor("dbg", [128, 5120], BF16, kind="ExternalOutput").ap()
             if dbg else None)
    dbgA_d = (nc.dram_tensor("dbgA", [128, 2560], F32, kind="ExternalOutput").ap()
              if dbg2 else None)
    dbgB_d = (nc.dram_tensor("dbgB", [128, 2048], BF16, kind="ExternalOutput").ap()
              if dbg2 else None)

    wbf_win = nc.dram_tensor("wbf_win", [DM, 2 * DI], BF16, kind="Internal").ap()
    wbf_w1 = nc.dram_tensor("wbf_w1", [DM, HI], BF16, kind="Internal").ap()
    wbf_w2 = nc.dram_tensor("wbf_w2", [HI, DM], BF16, kind="Internal").ap()
    wbf_wout = nc.dram_tensor("wbf_wout", [DI, DM], BF16, kind="Internal").ap()
    xp_sc = nc.dram_tensor("xp_sc", [NCH, 32, C], BF16, kind="Internal").ap()
    ident_dram = nc.inline_tensor(np.eye(128, dtype=np.float32), name="ident128")

    es = ExitStack()
    E = es.enter_context

    xa = E(nc.sbuf_tensor("xa", [128, MT * DM], F32))
    lnxtm = E(nc.sbuf_tensor("lnxtm", [128, DM], F32))
    stats = E(nc.sbuf_tensor("stats", [128, 32], F32))
    lnxT = E(nc.sbuf_tensor("lnxT", [128, KC * C], BF16))
    xsT = E(nc.sbuf_tensor("xsT", [128, DC * C], BF16))
    szT = E(nc.sbuf_tensor("szT", [128, DC * C], BF16))
    xpTb = [E(nc.sbuf_tensor(f"xpTb{i}", [96, C], BF16)) for i in range(2)]
    arena = E(nc.sbuf_tensor("arena", [128, 2 * DC * C], BF16))
    delta = [E(nc.sbuf_tensor(f"delta{i}", [128, C], F32)) for i in range(2)]
    tmp1 = [E(nc.sbuf_tensor(f"tmp1{i}", [128, C], F32)) for i in range(2)]
    tmp2 = [E(nc.sbuf_tensor(f"tmp2{i}", [128, C], F32)) for i in range(2)]
    aTq = [E(nc.sbuf_tensor(f"aTq{i}", [128, 2 * C], F32)) for i in range(2)]
    bTq0 = E(nc.sbuf_tensor("bTq0", [128, 2 * C], BF16))
    bTq = [bTq0, bTq0]
    hT = E(nc.sbuf_tensor("hT", [128, DC * C], BF16))
    hcar = E(nc.sbuf_tensor("hcar", [128, DC * NS], F32))
    yf = E(nc.sbuf_tensor("yf", [128, C], F32))
    dxd = E(nc.sbuf_tensor("dxd", [128, C], BF16))
    ygT = [E(nc.sbuf_tensor(f"ygT{i}", [128, DC * C], BF16)) for i in range(2)]
    pool = E(nc.sbuf_tensor("pool", [128, 8192], BF16))
    xres = [E(nc.sbuf_tensor(f"xres{i}", [128, 512], F32)) for i in range(2)]
    outb = [E(nc.sbuf_tensor(f"outb{i}", [128, 512], F32)) for i in range(2)]
    WxB = E(nc.sbuf_tensor("WxB", [128, DC * 96], BF16))
    WdtB = E(nc.sbuf_tensor("WdtB", [RK, DI], BF16))
    A_sb = E(nc.sbuf_tensor("A_sb", [128, DC * NS], F32))
    Dp_sb = E(nc.sbuf_tensor("Dp_sb", [128, DC], F32))
    bin_sb = E(nc.sbuf_tensor("bin_sb", [128, 32], F32))
    bdt_sb = E(nc.sbuf_tensor("bdt_sb", [128, DC], F32))
    b1_sb = E(nc.sbuf_tensor("b1_sb", [128, HC], F32))
    bx_sb = E(nc.sbuf_tensor("bx_sb", [96, 1], F32))
    boutrow = E(nc.sbuf_tensor("boutrow", [1, DM], BF16))
    b2row = E(nc.sbuf_tensor("b2row", [1, DM], BF16))
    onesrow = E(nc.sbuf_tensor("onesrow", [1, 128], BF16))
    identf = E(nc.sbuf_tensor("identf", [128, 128], F32))

    sbuf_left = nc.sbuf_bytes_remaining
    if report is not None:
        report["sbuf_left"] = sbuf_left

    pt = [E(nc.psum_tensor(f"pt{i}", [128, 512], F32)) for i in range(6)]
    ptr = [E(nc.psum_tensor(f"ptr{i}", [128, 512], F32)) for i in range(2)]

    sem_names = ["sT", "sA", "sV"] + [f"lk_{e}" for e in ENGS]
    for cls, nsem in DMA_POOLS.items():
        sem_names += [f"{cls}{k}" for k in range(nsem)]
    sems = {n: E(nc.semaphore(n, num=220 + k))
            for k, n in enumerate(sem_names)}

    Brep = arena[:, 0:DC * C]
    Crep = arena[:, DC * C:2 * DC * C]
    h1T = arena[:, 0:2 * DC * C]
    quarter = [pool[:, i * 2048:(i + 1) * 2048] for i in range(4)]

    def program(P: Prog):
        P.lk_budget = lockstep
        P.lk_skip = lockstep_skip
        def V(fn, sem=None, inc=1):
            P.op("vector", fn, sem, inc)

        def A(fn, sem=None, inc=1):
            P.op("scalar", fn, sem, inc)

        def TE(fn, sem=None, inc=1):
            P.op("tensor", fn, sem, inc)

        st = {
            "wdt_free": [0, 0],
            "tmp_free": [0, 0],
            "delta_free": [0, 0],
            "aTq_free": [0, 0],
            "ptr_free": [0, 0],
            "q_free": [0, 0, 0, 0],
            "ptwA": [0, 0, 0, 0],
            "ptwV": [0, 0, 0, 0],
            "gq": 0,
            "qrot": 0,
            "grot": 0,
            "xa_freeV": 0, "xa_freeA": 0,
            "xa_load": 0,
            "lnxT_free": 0,
            "lnxtm_free": 0,
            "xp_free": [0, 0],
            "ygT_free": 0,
            "bc_free": 0,
            "h1T_free": 0,
            "scanb": [[0] * DC, [0] * DC],
            "yg_done": 0,
            "setup_w": 0,
            "sq_free": 0,
            "xres_free": [0, 0],
            "outb_free": [None, None],
            "lnready": [0, 0],
        }

        if dbg2:
            P.dma("sync", dbgA_d[:, 0:512], Win_d[896:1024, 3584:4096], "w")
            P.dma("sync", dbgA_d[:, 512:1024], W1_d[896:1024, 3584:4096], "w")
            P.dma("sync", dbgA_d[:, 1024:1536], W2_d[3968:4096, 512:1024], "w")
            P.dma("sync", dbgA_d[:, 1536:2048], Wout_d[1920:2048, 512:1024], "w")
            P.dma("sync", dbgA_d[:, 2048:2560], x_d[3968:4096, 512:1024], "w")
        if mode == "slow2":
            for it in range(4000):
                t_ = P.dma("sync", identf[:], ident_dram.ap(), "w")
                P.wait_t("sync", t_)
            for e in ENGS:
                P.wait_all(e, P.mark_all("w"))
        for fn in (AF.Silu, AF.Exp, AF.Ln, AF.Square, AF.Abs, AF.Relu,
                   AF.Identity, AF.Copy):
            A(lambda sc, fn=fn: sc.activation(out=stats[:, 30:31],
                                              in_=stats[:, 31:32], func=fn))
        A(lambda sc: sc.nop(), sem="sA")
        if report is not None and hasattr(P, "n"):
            report["op_warm"] = P.n
        warm_mark = P.mark("sA")
        for e in ("sync", "vector", "tensor", "gpsimd"):
            P.wait(e, "sA", warm_mark)
        P.dma("sync", identf[:], ident_dram.ap(), "w")
        with nc.allow_non_contiguous_dma("small constant layout loads"):
            P.dma("sync", bx_sb[:], bx_d.rearrange("(p o) -> p o", o=1), "w")
            P.dma("sync", bin_sb[:], bin_d.rearrange("(m p) -> p m", p=128), "w")
            P.dma("sync", bdt_sb[:], bdt_d.rearrange("(m p) -> p m", p=128), "w")
            P.dma("sync", b1_sb[:], b1_d.rearrange("(m p) -> p m", p=128), "w")
            P.dma("sync", Dp_sb[:], Dp_d.rearrange("(m p) -> p m", p=128), "w")
            P.dma("sync", A_sb[:].rearrange("p (m n) -> p m n", n=NS),
                  Alog_d.rearrange("(m p) n -> p m n", p=128), "w")
        P.dma("sync", lnxtm[0:1, 0:DM], bout_d.rearrange("(o m) -> o m", o=1), "w")
        hTf = hT[:].bitcast(F32)
        P.dma("sync", hTf[:, 0:DC * 96].rearrange("p (m n) -> p m n", n=96),
              Wx_d.rearrange("(m p) n -> p m n", p=128), "w")
        P.dma("sync", hTf[0:RK, DC * 96:DC * 96 + DI], Wdt_d, "w")
        const_w = P.mark_all("w")
        if report is not None and hasattr(P, "n"):
            report["op_const"] = P.n

        P.wait_all("vector", const_w)
        V(lambda v: v.memset(onesrow[:], 1.0))
        V(lambda v: v.memset(hcar[:], 0.0))
        V(lambda v: v.memset(stats[:, 28:29], 1e-5))
        V(lambda v: v.tensor_copy(out=boutrow[:], in_=lnxtm[0:1, 0:DM]), sem="sV")
        P.wait("sync", "sV")
        tb2 = P.dma("sync", lnxtm[0:1, 0:DM],
                    b2_d.rearrange("(o m) -> o m", o=1), "w")
        P.wait_t("vector", tb2)
        V(lambda v: v.tensor_copy(out=b2row[:], in_=lnxtm[0:1, 0:DM]), sem="sV")
        P.wait_all("scalar", const_w)
        A(lambda s: s.activation(out=A_sb[:], in_=A_sb[:], func=AF.Exp), sem="sA")
        P.wait("vector", "sA")
        V(lambda v: v.tensor_single_scalar(out=A_sb[:], in_=A_sb[:], scalar=-1.0,
                                           op=ALU.mult), sem="sV")
        A(lambda s: s.activation(out=WxB[:], in_=hTf[:, 0:DC * 96], func=AF.Copy))
        A(lambda s: s.activation(out=WdtB[:], in_=hTf[0:RK, DC * 96:DC * 96 + DI],
                                 func=AF.Copy), sem="sA")
        prev_cast = [P.mark("sA"), P.mark("sA")]
        if report is not None and hasattr(P, "n"):
            report["op_wxwdt"] = P.n

        rounds = []
        for r in range(8):
            for cc in range(2):
                rounds.append((wbf_win, r, cc * 2048, 2048, Win_d))
        for r in range(8):
            for cc in range(2):
                rounds.append((wbf_w1, r, cc * 2048, 2048, W1_d))
        for r in range(16):
            rounds.append((wbf_wout, r, 0, DM, Wout_d))
        for r in range(32):
            rounds.append((wbf_w2, r, 0, DM, W2_d))
        prev_store = [None, None]
        for ri, (dst, r, c0, nc_, src) in enumerate(rounds):
            hb = ri % 2
            fview = hTf[:, hb * 2048:hb * 2048 + nc_]
            bview = szT[:, hb * 2048:hb * 2048 + nc_]
            P.wait("sync", "sA", prev_cast[hb])
            ld = P.dma("sync", fview, src[r * 128:(r + 1) * 128, c0:c0 + nc_], "w")
            P.wait_t("scalar", ld)
            P.wait_t("scalar", prev_store[hb])
            A(lambda s, fview=fview, bview=bview: s.activation(
                out=bview, in_=fview, func=AF.Copy), sem="sA")
            cm = P.mark("sA")
            P.wait("sync", "sA", cm)
            prev_store[hb] = P.dma(
                "sync", dst[r * 128:(r + 1) * 128, c0:c0 + nc_], bview, "w")
            prev_cast[hb] = cm
        st["setup_w"] = P.mark_all("w")
        if report is not None and hasattr(P, "n"):
            report["op_rounds"] = P.n

        def ln_tile(i, ld_ticket, do_tp=True):
            xai = xa[:, i * DM:(i + 1) * DM]
            if isinstance(ld_ticket, list):
                for t_ in ld_ticket:
                    P.wait_t("vector", t_)
                    P.wait_t("scalar", t_)
                ld_ticket = ld_ticket[-1]
            P.wait_t("vector", ld_ticket)
            V(lambda v, xai=xai, i=i: v.tensor_reduce(
                out=stats[:, i:i + 1], in_=xai, axis=AX.X, op=ALU.add), sem="sV")
            P.wait_t("scalar", ld_ticket)
            P.wait("scalar", "sV", st["sq_free"])
            P.wait("scalar", "sT", st["lnxtm_free"])
            A(lambda s, xai=xai: s.activation(out=lnxtm[:], in_=xai,
                                              func=AF.Square), sem="sA")
            sq_mark = P.mark("sA")
            P.wait("vector", "sA", sq_mark)
            V(lambda v, i=i: v.tensor_reduce(
                out=stats[:, 4 + i:5 + i], in_=lnxtm[:], axis=AX.X, op=ALU.add))
            V(lambda v, i=i: v.tensor_single_scalar(
                out=stats[:, 8 + i:9 + i], in_=stats[:, i:i + 1],
                scalar=1.0 / DM, op=ALU.mult))
            V(lambda v, i=i: v.tensor_mul(
                out=stats[:, 12 + i:13 + i], in0=stats[:, 8 + i:9 + i],
                in1=stats[:, 8 + i:9 + i]))
            V(lambda v, i=i: v.tensor_scalar(
                out=stats[:, 16 + i:17 + i], in0=stats[:, 4 + i:5 + i],
                scalar1=1.0 / DM, scalar2=stats[:, 12 + i:13 + i],
                op0=ALU.mult, op1=ALU.subtract), sem="sV")
            st["sq_free"] = P.mark("sV")
            P.wait("scalar", "sV", st["sq_free"])
            A(lambda s, i=i: s.activation(out=stats[:, 20 + i:21 + i],
                                          in_=stats[:, 16 + i:17 + i],
                                          func=AF.Ln, bias=stats[:, 28:29]))
            A(lambda s, i=i: s.activation(out=stats[:, 24 + i:25 + i],
                                          in_=stats[:, 20 + i:21 + i],
                                          func=AF.Exp, scale=-0.5), sem="sA")
            inv_mark = P.mark("sA")
            if not do_tp:
                return
            P.wait("vector", "sA", inv_mark)
            V(lambda v, xai=xai, i=i: v.tensor_scalar(
                out=lnxtm[:], in0=xai,
                scalar1=stats[:, 8 + i:9 + i], scalar2=stats[:, 24 + i:25 + i],
                op0=ALU.subtract, op1=ALU.mult), sem="sV")
            lnx_mark = P.mark("sV")
            st["sq_free"] = lnx_mark
            lnv = lnxT[:].rearrange("p (kc c) -> p kc c", c=C)
            for hh in range(2):
                b = (i * 2 + hh) % 2
                P.wait("tensor", "sV", lnx_mark)
                P.wait("tensor", "sV", st["ptr_free"][b])
                for j in range(4):
                    kc = hh * 4 + j
                    TE(lambda t, kc=kc, j=j, b=b: t.transpose(
                        ptr[b][:, j * 128:(j + 1) * 128],
                        lnxtm[:, kc * 128:(kc + 1) * 128], identf[:]),
                       sem=("sT" if j == 3 else None))
                if hh == 1:
                    st["lnxtm_free"] = P.mark("sT")
                tp_mark = P.mark("sT")
                P.wait("vector", "sT", tp_mark)
                if i == 0 and hh == 0:
                    P.wait("vector", "sT", st["lnxT_free"])
                V(lambda v, hh=hh, i=i, b=b, lnv=lnv: v.tensor_copy(
                    out=lnv[:, hh * 4:(hh + 1) * 4, i * 128:(i + 1) * 128],
                    in_=ptr[b][:].rearrange("p (j c) -> p j c", c=128)),
                  sem="sV")
                st["ptr_free"][b] = P.mark("sV")
            if i == MT - 1:
                st["xa_freeV"] = P.mark("sV")
                st["xa_freeA"] = P.mark("sA")

        def ln_apply(i):
            xai = xa[:, i * DM:(i + 1) * DM]
            P.wait("vector", "sT", st["lnxtm_free"])
            V(lambda v, xai=xai, i=i: v.tensor_scalar(
                out=lnxtm[:], in0=xai,
                scalar1=stats[:, 8 + i:9 + i], scalar2=stats[:, 24 + i:25 + i],
                op0=ALU.subtract, op1=ALU.mult), sem="sV")
            lnx_mark = P.mark("sV")
            st["sq_free"] = lnx_mark
            lnv = lnxT[:].rearrange("p (kc c) -> p kc c", c=C)
            for hh in range(2):
                b = (i * 2 + hh) % 2
                P.wait("tensor", "sV", lnx_mark)
                P.wait("tensor", "sV", st["ptr_free"][b])
                for j in range(4):
                    kc = hh * 4 + j
                    TE(lambda t, kc=kc, j=j, b=b: t.transpose(
                        ptr[b][:, j * 128:(j + 1) * 128],
                        lnxtm[:, kc * 128:(kc + 1) * 128], identf[:]),
                       sem=("sT" if j == 3 else None))
                if hh == 1:
                    st["lnxtm_free"] = P.mark("sT")
                tp_mark = P.mark("sT")
                P.wait("vector", "sT", tp_mark)
                if i == 0 and hh == 0:
                    P.wait("vector", "sT", st["lnxT_free"])
                V(lambda v, hh=hh, i=i, b=b, lnv=lnv: v.tensor_copy(
                    out=lnv[:, hh * 4:(hh + 1) * 4, i * 128:(i + 1) * 128],
                    in_=ptr[b][:].rearrange("p (j c) -> p j c", c=128)),
                  sem="sV")
                st["ptr_free"][b] = P.mark("sV")
            if i == MT - 1:
                st["xa_freeV"] = P.mark("sV")
                st["xa_freeA"] = P.mark("sA")

        def stream_quarter(wbf, jq, kc0=0):
            qb = st["qrot"] % 4
            st["qrot"] += 1
            P.wait("sync", "sT", st["q_free"][qb])
            for k8 in range(8):
                kc = kc0 + k8
                tk = P.dma("sync", quarter[qb][:, k8 * 256:(k8 + 1) * 256],
                           wbf[kc * 128:(kc + 1) * 128,
                               jq * 256:(jq + 1) * 256], "w")
            return qb, tk

        def front_quarter(wbf, jq, lnready, spar, evac):
            qb, slab_mark = stream_quarter(wbf, jq)
            for m in range(2):
                mi = jq * 2 + m
                bank = mi % 4
                P.wait_t("tensor", slab_mark)
                P.wait("tensor", "sA", st["ptwA"][bank])
                P.wait("tensor", "sV", st["ptwV"][bank])
                P.wait("tensor", "sV", lnready)
                for kc in range(KC):
                    TE(lambda t, kc=kc, m=m, qb=qb, bank=bank: t.matmul(
                        pt[bank][:],
                        quarter[qb][:, kc * 256 + m * 128:kc * 256 + (m + 1) * 128],
                        lnxT[:, kc * C:(kc + 1) * C],
                        start=(kc == 0), stop=(kc == KC - 1)),
                       sem=("sT" if kc == KC - 1 else None))
                mm_mark = P.mark("sT")
                P.wait("scalar", "sT", mm_mark)
                evac(mi, bank, spar)
                st["ptwA"][bank] = P.mark("sA")
            st["q_free"][qb] = P.mark("sT")

        def silu_evac(mi, bank, spar):
            if mi < DC:
                dst, di = xsT, mi
            else:
                dst, di = szT, mi - DC
            if spar is not None:
                P.wait("scalar", "sV", st["scanb"][spar][di])
            A(lambda s, dst=dst, di=di, bank=bank, mi=mi: s.activation(
                out=dst[:, di * C:(di + 1) * C], in_=pt[bank][:],
                func=AF.Silu, bias=bin_sb[:, mi:mi + 1]), sem="sA")

        def relu_evac(mi, bank, spar):
            if mi == 0:
                P.wait("scalar", "sT", st["h1T_free"])
            A(lambda s, mi=mi, bank=bank: s.activation(
                out=h1T[:, mi * C:(mi + 1) * C], in_=pt[bank][:],
                func=AF.Relu, bias=b1_sb[:, mi:mi + 1]), sem="sA")

        def xp_phase(ch):
            par = ch % 2
            P.wait("tensor", "sA")
            P.wait("tensor", "sT", st["xp_free"][par])
            P.wait("tensor", "sV", st["ptr_free"][0])
            for kc in range(DC):
                TE(lambda t, kc=kc: t.matmul(
                    ptr[0][0:96, :], WxB[:, kc * 96:(kc + 1) * 96],
                    xsT[:, kc * C:(kc + 1) * C],
                    start=(kc == 0), stop=(kc == DC - 1)),
                   sem=("sT" if kc == DC - 1 else None))
            xp_mark = P.mark("sT")
            P.wait("vector", "sT", xp_mark)
            V(lambda v, par=par: v.tensor_scalar(
                out=xpTb[par][:], in0=ptr[0][0:96, :], scalar1=bx_sb[:],
                scalar2=None, op0=ALU.add), sem="sV")
            if dbg and ch == 0:
                V(lambda v, par=par: v.tensor_copy(out=ygT[1][:, 4096:4608][0:96],
                                                   in_=xpTb[par][:]))
            xp_done = P.mark("sV")
            st["ptr_free"][0] = xp_done
            P.wait("gpsimd", "sV", xp_done)
            P.wait("gpsimd", "sV", st["bc_free"])
            twr = P.dma("gpsimd", xp_sc[ch], xpTb[par][64:96, :], "bc")
            P.wait_t("gpsimd", twr)
            tb = P.dma("gpsimd", Brep.rearrange("p (n c) -> p n c", c=C),
                       xp_sc[ch, 0:NS, :].unsqueeze(0)
                       .broadcast_to([128, NS, C]), "bc")
            tc = P.dma("gpsimd", Crep.rearrange("p (n c) -> p n c", c=C),
                       xp_sc[ch, NS:2 * NS, :].unsqueeze(0)
                       .broadcast_to([128, NS, C]), "bc")
            return xp_done, (tb, tc)

        def scan_dc(ch, dc, xp_done, bc_done):
            par = dc % 2
            cpar = ch % 2
            P.wait("tensor", "sA", st["wdt_free"][par])
            P.wait("tensor", "sV", xp_done)
            TE(lambda t, dc=dc, par=par, cpar=cpar: t.matmul(
                pt[4 + par][:], WdtB[:, dc * 128:(dc + 1) * 128],
                xpTb[cpar][0:RK, :], start=True, stop=True), sem="sT")
            wdt_mark = P.mark("sT")
            if dc == DC - 1:
                st["xp_free"][cpar] = wdt_mark
            P.wait("scalar", "sT", wdt_mark)
            P.wait("scalar", "sV", st["tmp_free"][par])
            A(lambda s, dc=dc, par=par: s.activation(
                out=tmp1[par][:], in_=pt[4 + par][:], func=AF.Abs,
                bias=bdt_sb[:, dc:dc + 1]))
            A(lambda s, par=par: s.activation(
                out=tmp2[par][:], in_=tmp1[par][:], func=AF.Exp, scale=-1.0))
            A(lambda s, par=par: s.activation(
                out=tmp1[par][:], in_=tmp2[par][:], func=AF.Ln, bias=1.0))
            A(lambda s, dc=dc, par=par: s.activation(
                out=tmp2[par][:], in_=pt[4 + par][:], func=AF.Relu,
                bias=bdt_sb[:, dc:dc + 1]), sem="sA")
            if dbg and ch == 0 and dc == 1:
                A(lambda s, cpar=cpar: s.activation(
                    out=ygT[1][:, 4608:5120][0:96], in_=xpTb[cpar][:],
                    func=AF.Copy))
                A(lambda s, par=par: s.activation(
                    out=ygT[1][:, 2048:2560], in_=tmp1[par][:], func=AF.Copy))
                A(lambda s, par=par: s.activation(
                    out=ygT[1][:, 2560:3072], in_=tmp2[par][:], func=AF.Copy))
                A(lambda s, par=par: s.activation(
                    out=ygT[1][:, 1024 + 512:1024 + 1024],
                    in_=pt[4 + par][:], func=AF.Copy))
            st["wdt_free"][par] = P.mark("sA")
            sp_mark = P.mark("sA")
            P.wait("vector", "sA", sp_mark)
            P.wait("vector", "sA", st["delta_free"][par])
            if ch == 0 and dc == 0:
                P.wait_all("vector", st["setup_w"])
            V(lambda v, par=par: v.tensor_add(
                out=delta[par][:], in0=tmp1[par][:], in1=tmp2[par][:]))
            V(lambda v, dc=dc, par=par: v.tensor_mul(
                out=dxd[:], in0=delta[par][:],
                in1=xsT[:, dc * C:(dc + 1) * C]), sem="sV")
            st["tmp_free"][par] = P.mark("sV")
            delta_mark = P.mark("sV")
            if dbg and ch == 0 and dc == 1:
                V(lambda v, par=par: v.tensor_copy(out=ygT[1][:, 0:512],
                                                   in_=delta[par][:]))
                V(lambda v: v.tensor_copy(out=ygT[1][:, 512:1024], in_=dxd[:]))

            for q in range(8):
                qb = st["gq"] % 2
                P.wait("scalar", "sV", delta_mark)
                P.wait("scalar", "sV", st["aTq_free"][qb])
                for j in range(2):
                    n = q * 2 + j
                    A(lambda s, dc=dc, n=n, j=j, par=par, qb=qb: s.activation(
                        out=aTq[qb][:, j * C:(j + 1) * C], in_=delta[par][:],
                        func=AF.Exp,
                        scale=A_sb[:, dc * NS + n:dc * NS + n + 1]),
                      sem=("sA" if j == 1 else None))
                aexp_mark = P.mark("sA")
                if dc == 0 and q == 0:
                    P.wait_t("vector", bc_done[0])
                    P.wait_t("vector", bc_done[1])
                for j in range(2):
                    n = q * 2 + j
                    V(lambda v, n=n, j=j, qb=qb: v.tensor_mul(
                        out=bTq[qb][:, j * C:(j + 1) * C], in0=dxd[:],
                        in1=Brep[:, n * C:(n + 1) * C]))
                P.wait("vector", "sA", aexp_mark)
                if dbg and ch == 0 and dc == 1 and q == 0:
                    V(lambda v, qb=qb: v.tensor_copy(out=ygT[1][:, 1024:1536],
                                                     in_=aTq[qb][:, 0:512]))
                    V(lambda v: v.tensor_copy(out=ygT[1][:, 3072:3584],
                                              in_=Brep[:, 0:512]))
                for j in range(2):
                    n = q * 2 + j
                    V(lambda v, dc=dc, n=n, j=j, qb=qb: v.tensor_tensor_scan(
                        out=hT[:, n * C:(n + 1) * C],
                        data0=aTq[qb][:, j * C:(j + 1) * C],
                        data1=bTq[qb][:, j * C:(j + 1) * C],
                        initial=hcar[:, dc * NS + n:dc * NS + n + 1],
                        op0=ALU.mult, op1=ALU.add),
                      sem=("sV" if j == 1 else None))
                st["aTq_free"][qb] = P.mark("sV")
                st["gq"] += 1
            st["delta_free"][par] = P.mark("sA")

            hv = hT[:].rearrange("p (n c) -> p n c", c=C)
            V(lambda v, dc=dc, hv=hv: v.tensor_copy(
                out=hcar[:, dc * NS:(dc + 1) * NS]
                .rearrange("p (n o) -> p n o", o=1),
                in_=hv[:, :, C - 1:C]))
            V(lambda v: v.tensor_mul(out=hT[:], in0=hT[:], in1=Crep[:]))
            V(lambda v: v.tensor_add(out=hT[:, 0:8 * C], in0=hT[:, 0:8 * C],
                                     in1=hT[:, 8 * C:16 * C]))
            V(lambda v: v.tensor_add(out=hT[:, 0:4 * C], in0=hT[:, 0:4 * C],
                                     in1=hT[:, 4 * C:8 * C]))
            V(lambda v: v.tensor_add(out=hT[:, 0:2 * C], in0=hT[:, 0:2 * C],
                                     in1=hT[:, 2 * C:4 * C]))
            V(lambda v: v.tensor_add(out=yf[:], in0=hT[:, 0:C],
                                     in1=hT[:, C:2 * C]))
            V(lambda v, dc=dc: v.scalar_tensor_tensor(
                out=yf[:], in0=xsT[:, dc * C:(dc + 1) * C],
                scalar=Dp_sb[:, dc:dc + 1], in1=yf[:],
                op0=ALU.mult, op1=ALU.add))
            if dc == 0:
                P.wait("vector", "sT", st["ygT_free"])
            V(lambda v, dc=dc, cpar=cpar: v.tensor_mul(
                out=ygT[cpar][:, dc * C:(dc + 1) * C], in0=yf[:],
                in1=szT[:, dc * C:(dc + 1) * C]), sem="sV")
            if dbg and ch == 0 and dc == 1:
                V(lambda v: v.tensor_copy(out=ygT[1][:, 3584:4096],
                                          in_=hT[:, 0:512]), sem="sV")
            if dbg and ch == 0 and dc == DC - 1:
                V(lambda v: v.tensor_copy(
                    out=ygT[1][:, 2048:2304].bitcast(F32)[:, 0:128]
                    if False else ygT[1][:, 1024:1280],
                    in_=hcar[:, 0:256]))
                yv = ygT[0][:].rearrange("p (d c) -> p d c", c=C)
                V(lambda v, yv=yv: v.tensor_copy(
                    out=ygT[1][:, 0:512].rearrange("p (d c) -> p d c", c=32),
                    in_=yv[:, :, 48:80]), sem="sV")
                dm = P.mark("sV")
                P.wait("gpsimd", "sV", dm)
                P.dma("gpsimd", dbg_d, ygT[1][:, 0:5120], "o")
            st["scanb"][cpar][dc] = P.mark("sV")
            if dc == DC - 1:
                st["yg_done"] = P.mark("sV")
                st["bc_free"] = st["yg_done"]

        def out_group(ch, gi, n_pieces, wbf, lhsT_buf, ready_sV, ready_sA,
                      resid_src, biasrow, last):
            ncol_groups = DM // 256
            mt = gi // ncol_groups
            cg = gi % ncol_groups
            col0 = cg * 256
            bank = gi % 4
            gb = st["grot"] % 2
            st["grot"] += 1
            P.wait("gpsimd", "sV", st["xres_free"][gb])
            xres_mark = P.dma("gpsimd", xres[gb][:, 0:256],
                              resid_src[ch * C + mt * 128:ch * C + (mt + 1) * 128,
                                        col0:col0 + 256], "x")
            pieces = []
            for pc in range(n_pieces):
                qb, ldm = stream_quarter(wbf, cg, pc * 8)
                pieces.append((qb, ldm))
            P.wait("tensor", "sV", ready_sV)
            P.wait("tensor", "sA", ready_sA)
            P.wait("tensor", "sA", st["ptwA"][bank])
            P.wait("tensor", "sV", st["ptwV"][bank])
            for pc, (qb, ldm) in enumerate(pieces):
                P.wait_t("tensor", ldm)
                for k8 in range(8):
                    kc = pc * 8 + k8
                    TE(lambda t, kc=kc, mt=mt, qb=qb, k8=k8, bank=bank: t.matmul(
                        pt[bank][:, 0:256],
                        lhsT_buf[:, kc * C + mt * 128:kc * C + (mt + 1) * 128],
                        quarter[qb][:, k8 * 256:(k8 + 1) * 256],
                        start=(kc == 0), stop=False))
            TE(lambda t, col0=col0, bank=bank: t.matmul(
                pt[bank][:, 0:256], onesrow[:], biasrow[:, col0:col0 + 256],
                start=False, stop=True), sem="sT")
            for qb, _ in pieces:
                st["q_free"][qb] = P.mark("sT")
            mm_mark = P.mark("sT")
            P.wait("vector", "sT", mm_mark)
            P.wait_t("vector", xres_mark)
            P.wait_t("vector", st["outb_free"][gb])
            V(lambda v, bank=bank, gb=gb: v.scalar_tensor_tensor(
                out=outb[gb][:, 0:256], in0=pt[bank][:, 0:256], scalar=1.0,
                in1=xres[gb][:, 0:256], op0=ALU.mult, op1=ALU.add), sem="sV")
            ev = P.mark("sV")
            st["xres_free"][gb] = ev
            st["ptwV"][bank] = ev
            P.wait("gpsimd", "sV", ev)
            st["outb_free"][gb] = P.dma(
                "gpsimd",
                out_d[ch * C + mt * 128:ch * C + (mt + 1) * 128, col0:col0 + 256],
                outb[gb][:, 0:256], "o")
            if last:
                if wbf is wbf_wout:
                    st["ygT_free"] = mm_mark
                else:
                    st["h1T_free"] = mm_mark

        def load_chunk(ch, src):
            P.wait("gpsimd", "sV", st["xa_freeV"])
            P.wait("gpsimd", "sA", st["xa_freeA"])
            st["xa_load"] = [
                P.dma("gpsimd", xa[:, i * DM:(i + 1) * DM],
                      src[ch * C + i * 128:ch * C + (i + 1) * 128, :], "x")
                for i in range(MT)]

        def front_serial(nxt, ch, src):
            load_chunk(nxt, src)
            for i in range(MT):
                ln_tile(i, list(st["xa_load"]))
            st["lnready"][nxt % 2] = P.mark("sV")
            for j in range(6):
                front_quarter(wbf_win, j, st["lnready"][nxt % 2],
                              ch % 2, silu_evac)
                front_quarter(wbf_win, 8 + j, st["lnready"][nxt % 2],
                              ch % 2, silu_evac)

        if mode == "slow":
            for it in range(200):
                t_ = P.dma("sync", identf[:], ident_dram.ap(), "w")
                P.wait_t("sync", t_)
        for e in ENGS:
            P.wait_all(e, st["setup_w"])
            P.wait(e, "sA")
            P.wait(e, "sV")
        P.wait("scalar", "sV")
        load_chunk(0, x_d)
        if lockstep == 0 and lockstep_skip == 0:
            P.lk_budget = 10 ** 6
        for i in range(MT):
            ln_tile(i, list(st["xa_load"]))
        st["lnready"][0] = P.mark("sV")
        P.wait_all("scalar", st["setup_w"])
        for jq in range(8):
            front_quarter(wbf_win, jq, st["lnready"][0], None, silu_evac)
            front_quarter(wbf_win, 8 + jq, st["lnready"][0], None, silu_evac)
        st["lnxT_free"] = P.mark("sT")
        xp_done, bc_done = xp_phase(0)

        interleave_front = (not serial) and mode in ("full", "front", "fln", "flstats")
        if mode == "barrier":
            interleave_front = False
        interleave_win = (not serial) and mode in ("full", "front")
        interleave_wout = (not serial) and mode in ("full", "wout")
        if mode == "barrier":
            interleave_wout = False
        yg_done_by_ch = {}

        def wout_all(ch):
            for gi in range(16):
                out_group(ch, gi, 2, wbf_wout, ygT[ch % 2],
                          yg_done_by_ch[ch], 0, x_d, boutrow, last=(gi == 15))

        for ch in range(NCH):
            nxt = ch + 1 if ch + 1 < NCH else None
            for dc in range(DC):
                scan_dc(ch, dc, xp_done, bc_done)
                if interleave_front and nxt is not None:
                    if dc == 0:
                        load_chunk(nxt, x_d)
                    if 1 <= dc <= 4:
                        ln_tile(dc - 1, st["xa_load"][dc - 1],
                                do_tp=(mode != "flstats"))
                        if dc == 4 and mode != "flstats":
                            st["lnready"][nxt % 2] = P.mark("sV")
                    if interleave_win and dc >= 5 and dc % 2 == 1:
                        j = (dc - 5) // 2
                        if j < 6:
                            front_quarter(wbf_win, j, st["lnready"][nxt % 2],
                                          ch % 2, silu_evac)
                            front_quarter(wbf_win, 8 + j, st["lnready"][nxt % 2],
                                          ch % 2, silu_evac)
                if interleave_wout and ch >= 1 and 2 <= dc <= 9:
                    for g in range(2):
                        gi = (dc - 2) * 2 + g
                        out_group(ch - 1, gi, 2, wbf_wout, ygT[(ch - 1) % 2],
                                  yg_done_by_ch[ch - 1], 0, x_d, boutrow,
                                  last=(gi == 15))
            yg_done_by_ch[ch] = st["yg_done"]
            if ch == 0 and lockstep == 0 and lockstep_skip == 0:
                P.lk_budget = 0
            if not interleave_wout and ch >= 1:
                wout_all(ch - 1)
            if nxt is not None:
                if not interleave_front:
                    front_serial(nxt, ch, x_d)
                elif mode == "flstats":
                    for i in range(MT):
                        ln_apply(i)
                    st["lnready"][nxt % 2] = P.mark("sV")
                    for j in range(6):
                        front_quarter(wbf_win, j, st["lnready"][nxt % 2],
                                      ch % 2, silu_evac)
                        front_quarter(wbf_win, 8 + j, st["lnready"][nxt % 2],
                                      ch % 2, silu_evac)
                elif not interleave_win:
                    for j in range(6):
                        front_quarter(wbf_win, j, st["lnready"][nxt % 2],
                                      ch % 2, silu_evac)
                        front_quarter(wbf_win, 8 + j, st["lnready"][nxt % 2],
                                      ch % 2, silu_evac)
                for j in (6, 7):
                    front_quarter(wbf_win, j, st["lnready"][nxt % 2],
                                  ch % 2, silu_evac)
                    front_quarter(wbf_win, 8 + j, st["lnready"][nxt % 2],
                                  ch % 2, silu_evac)
                st["lnxT_free"] = P.mark("sT")
                xp_done, bc_done = xp_phase(nxt)
        wout_all(NCH - 1)

        if stop_after == "passA":
            P.wait_all("gpsimd", P.mark_all("o"))
            return

        P.wait_all("gpsimd", P.mark_all("o"))
        load_chunk(0, out_d)
        for i in range(MT):
            ln_tile(i, list(st["xa_load"]))
        st["lnready"][0] = P.mark("sV")

        for ch in range(NCH):
            nxt = ch + 1 if ch + 1 < NCH else None
            for jq in range(16):
                front_quarter(wbf_w1, jq, st["lnready"][ch % 2], None, relu_evac)
                if jq == 0 and nxt is not None:
                    load_chunk(nxt, out_d)
            st["lnxT_free"] = P.mark("sT")
            relu_done = P.mark("sA")
            for gi in range(16):
                out_group(ch, gi, 4, wbf_w2, h1T, 0, relu_done, out_d, b2row,
                          last=(gi == 15))
                if nxt is not None and gi in (0, 1, 2, 3):
                    ln_tile(gi, list(st["xa_load"]))
                    if gi == 3:
                        st["lnready"][nxt % 2] = P.mark("sV")

        P.wait_all("gpsimd", P.mark_all("o"))
        if dbg2:
            P.dma("sync", dbgB_d[:, 0:512], wbf_win[896:1024, 3584:4096], "o")
            P.dma("sync", dbgB_d[:, 512:1024], wbf_w1[896:1024, 3584:4096], "o")
            P.dma("sync", dbgB_d[:, 1024:1536], wbf_w2[3968:4096, 512:1024], "o")
            P.dma("sync", dbgB_d[:, 1536:2048], wbf_wout[1920:2048, 512:1024], "o")
            P.wait_all("gpsimd", P.mark_all("o"))
            P.wait_all("gpsimd", P.mark_all("w"))

    if recorder is not None:
        program(recorder)
        es.close()
        return None, sbuf_left

    final_ctrs = []
    with nc.Block() as block:
        for name in ENGS:
            def section(eng_obj, name=name):
                p = Prog(name, eng_obj, sems)
                program(p)
                final_ctrs.append((name, dict(p.ctr)))
            getattr(block, name)(section)
    base = final_ctrs[0][1]
    for name, c in final_ctrs[1:]:
        assert c == base, ("walk divergence", name,
                           {k: (v, base[k]) for k, v in c.items() if v != base[k]})

    es.close()
    return nc, sbuf_left


_NC_CACHE = {}


def kernel(**inputs):
    x = np.ascontiguousarray(np.asarray(inputs["L0_attn_x"], dtype=np.float32))
    names = {
        "Win": "L0_attn_Win", "bin": "L0_attn_bin", "Wx": "L0_attn_Wx",
        "bx": "L0_attn_bx", "Wdt": "L0_attn_Wdt", "bdt": "L0_attn_bdt",
        "A_log": "L0_attn_A_log", "Dp": "L0_attn_Dp", "Wout": "L0_attn_Wout",
        "bout": "L0_attn_bout", "W1": "L0_ffn_W1", "b1": "L0_ffn_b1",
        "W2": "L0_ffn_W2", "b2": "L0_ffn_b2",
    }
    weights = {k: np.ascontiguousarray(np.asarray(inputs[v], dtype=np.float32))
               for k, v in names.items()}
    if "nc" not in _NC_CACHE:
        _NC_CACHE["nc"] = build_nc()[0]
    nc = _NC_CACHE["nc"]
    in_maps = [dict(weights, x=x[c]) for c in range(8)]
    res = run_bass_kernel_spmd(nc, in_maps, core_ids=list(range(8)))
    return np.stack([np.asarray(r["out"], dtype=np.float32) for r in res.results])


# version 08
# speedup vs baseline: 1.0352x; 1.0352x over previous
import numpy as np
from contextlib import ExitStack

import concourse.bass as bass
import concourse.mybir as mybir
from concourse.bass_utils import run_bass_kernel_spmd

F32 = mybir.dt.float32
BF16 = mybir.dt.bfloat16
AF = mybir.ActivationFunctionType
ALU = mybir.AluOpType
AX = mybir.AxisListType

T = 4096
DM = 1024
DI = 2048
NS = 16
RK = 64
HI = 4096
C = 512
NCH = T // C
MT = C // 128
KC = DM // 128
DC = DI // 128
HC = HI // 128

ENGS = ("sync", "tensor", "vector", "scalar", "gpsimd")


DMA_POOLS = {"x": 8, "w": 12, "bc": 3, "o": 4}


class Prog:

    def __init__(self, emit_eng, eng_obj, sems):
        self.emit = emit_eng
        self.e = eng_obj
        self.sems = sems
        self.ctr = {k: 0 for k in sems}
        self.rot = {k: 0 for k in DMA_POOLS}
        self.lk_budget = 0
        self.lk_skip = 0
        self.lk_dma = True

    def op(self, eng, fn, sem=None, inc=1):
        if self.lk_skip > 0:
            self.lk_skip -= 1
            lk = False
        else:
            lk = self.lk_budget > 0
        if lk:
            self.lk_budget -= 1
            if sem is None:
                sem = f"lk_{eng}"
        if eng == self.emit:
            ins = fn(self.e)
            if sem is not None:
                ins.then_inc(self.sems[sem], inc)
        if sem is not None:
            self.ctr[sem] += inc
            if lk:
                self.wait(self.emit, sem, self.ctr[sem])

    def dma(self, eng, out, in_, cls):
        k = self.rot[cls] % DMA_POOLS[cls]
        self.rot[cls] += 1
        sem = f"{cls}{k}"
        if self.ctr[sem] > 0:
            self.wait(eng, sem, self.ctr[sem])
        lk = self.lk_skip <= 0 and self.lk_budget > 0 and self.lk_dma
        b, self.lk_budget = self.lk_budget, 0
        self.op(eng, lambda e: e.dma_start(out=out, in_=in_), sem=sem, inc=16)
        self.lk_budget = (b - 1) if lk else b
        if lk:
            self.wait(self.emit, sem, self.ctr[sem])
        return (sem, self.ctr[sem])

    def wait_t(self, eng, ticket):
        if ticket is None:
            return
        sem, val = ticket
        self.wait(eng, sem, val)

    def mark_all(self, cls):
        return tuple((f"{cls}{k}", self.ctr[f"{cls}{k}"])
                     for k in range(DMA_POOLS[cls]))

    def wait_all(self, eng, marks):
        for sem, val in marks:
            self.wait(eng, sem, val)

    def wait(self, eng, sem, val=None):
        if val is None:
            val = self.ctr[sem]
        assert val <= self.ctr[sem], ("wait on future inc", sem, val, self.ctr[sem])
        if val <= 0:
            return
        if eng == self.emit:
            self.e.wait_ge(self.sems[sem], val)

    def mark(self, sem):
        return self.ctr[sem]


def build_nc(report=None, stop_after=None, serial=False, mode="full",
             recorder=None, dbg=False, dbg2=False, lockstep=0,
             lockstep_skip=0):
    nc = bass.Bass("TRN2", target_bir_lowering=False, debug=False)

    x_d = nc.dram_tensor("x", [T, DM], F32, kind="ExternalInput").ap()
    Win_d = nc.dram_tensor("Win", [DM, 2 * DI], F32, kind="ExternalInput").ap()
    bin_d = nc.dram_tensor("bin", [2 * DI], F32, kind="ExternalInput").ap()
    Wx_d = nc.dram_tensor("Wx", [DI, 96], F32, kind="ExternalInput").ap()
    bx_d = nc.dram_tensor("bx", [96], F32, kind="ExternalInput").ap()
    Wdt_d = nc.dram_tensor("Wdt", [RK, DI], F32, kind="ExternalInput").ap()
    bdt_d = nc.dram_tensor("bdt", [DI], F32, kind="ExternalInput").ap()
    Alog_d = nc.dram_tensor("A_log", [DI, NS], F32, kind="ExternalInput").ap()
    Dp_d = nc.dram_tensor("Dp", [DI], F32, kind="ExternalInput").ap()
    Wout_d = nc.dram_tensor("Wout", [DI, DM], F32, kind="ExternalInput").ap()
    bout_d = nc.dram_tensor("bout", [DM], F32, kind="ExternalInput").ap()
    W1_d = nc.dram_tensor("W1", [DM, HI], F32, kind="ExternalInput").ap()
    b1_d = nc.dram_tensor("b1", [HI], F32, kind="ExternalInput").ap()
    W2_d = nc.dram_tensor("W2", [HI, DM], F32, kind="ExternalInput").ap()
    b2_d = nc.dram_tensor("b2", [DM], F32, kind="ExternalInput").ap()
    out_d = nc.dram_tensor("out", [T, DM], F32, kind="ExternalOutput").ap()
    dbg_d = (nc.dram_tensor("dbg", [128, 5120], BF16, kind="ExternalOutput").ap()
             if dbg else None)
    dbgA_d = (nc.dram_tensor("dbgA", [128, 2560], F32, kind="ExternalOutput").ap()
              if dbg2 else None)
    dbgB_d = (nc.dram_tensor("dbgB", [128, 2048], BF16, kind="ExternalOutput").ap()
              if dbg2 else None)

    wbf_win = nc.dram_tensor("wbf_win", [DM, 2 * DI], BF16, kind="Internal").ap()
    wbf_w1 = nc.dram_tensor("wbf_w1", [DM, HI], BF16, kind="Internal").ap()
    wbf_w2 = nc.dram_tensor("wbf_w2", [HI, DM], BF16, kind="Internal").ap()
    wbf_wout = nc.dram_tensor("wbf_wout", [DI, DM], BF16, kind="Internal").ap()
    xp_sc = nc.dram_tensor("xp_sc", [NCH, 32, C], BF16, kind="Internal").ap()
    ident_dram = nc.inline_tensor(np.eye(128, dtype=np.float32), name="ident128")

    es = ExitStack()
    E = es.enter_context

    xa = E(nc.sbuf_tensor("xa", [128, MT * DM], F32))
    lnxtm = E(nc.sbuf_tensor("lnxtm", [128, DM], F32))
    stats = E(nc.sbuf_tensor("stats", [128, 32], F32))
    lnxT = E(nc.sbuf_tensor("lnxT", [128, KC * C], BF16))
    xsT = E(nc.sbuf_tensor("xsT", [128, DC * C], BF16))
    szT = E(nc.sbuf_tensor("szT", [128, DC * C], BF16))
    xpTb = [E(nc.sbuf_tensor(f"xpTb{i}", [96, C], BF16)) for i in range(2)]
    arena = E(nc.sbuf_tensor("arena", [128, 2 * DC * C], BF16))
    delta = [E(nc.sbuf_tensor(f"delta{i}", [128, C], F32)) for i in range(2)]
    tmp1 = [E(nc.sbuf_tensor(f"tmp1{i}", [128, C], F32)) for i in range(2)]
    tmp2 = [E(nc.sbuf_tensor(f"tmp2{i}", [128, C], F32)) for i in range(2)]
    aTq = [E(nc.sbuf_tensor(f"aTq{i}", [128, 2 * C], F32)) for i in range(2)]
    bTq0 = E(nc.sbuf_tensor("bTq0", [128, 2 * C], BF16))
    bTq = [bTq0, bTq0]
    hT = E(nc.sbuf_tensor("hT", [128, DC * C], BF16))
    hcar = E(nc.sbuf_tensor("hcar", [128, DC * NS], F32))
    yf = E(nc.sbuf_tensor("yf", [128, C], F32))
    dxd = E(nc.sbuf_tensor("dxd", [128, C], BF16))
    ygT = [E(nc.sbuf_tensor(f"ygT{i}", [128, DC * C], BF16)) for i in range(2)]
    pool = E(nc.sbuf_tensor("pool", [128, 8192], BF16))
    xres = [E(nc.sbuf_tensor(f"xres{i}", [128, 512], F32)) for i in range(2)]
    outb = [E(nc.sbuf_tensor(f"outb{i}", [128, 512], F32)) for i in range(2)]
    WxB = E(nc.sbuf_tensor("WxB", [128, DC * 96], BF16))
    WdtB = E(nc.sbuf_tensor("WdtB", [RK, DI], BF16))
    A_sb = E(nc.sbuf_tensor("A_sb", [128, DC * NS], F32))
    Dp_sb = E(nc.sbuf_tensor("Dp_sb", [128, DC], F32))
    bin_sb = E(nc.sbuf_tensor("bin_sb", [128, 32], F32))
    bdt_sb = E(nc.sbuf_tensor("bdt_sb", [128, DC], F32))
    b1_sb = E(nc.sbuf_tensor("b1_sb", [128, HC], F32))
    bx_sb = E(nc.sbuf_tensor("bx_sb", [96, 1], F32))
    boutrow = E(nc.sbuf_tensor("boutrow", [1, DM], BF16))
    b2row = E(nc.sbuf_tensor("b2row", [1, DM], BF16))
    onesrow = E(nc.sbuf_tensor("onesrow", [1, 128], BF16))
    identf = E(nc.sbuf_tensor("identf", [128, 128], F32))

    sbuf_left = nc.sbuf_bytes_remaining
    if report is not None:
        report["sbuf_left"] = sbuf_left

    pt = [E(nc.psum_tensor(f"pt{i}", [128, 512], F32)) for i in range(6)]
    ptr = [E(nc.psum_tensor(f"ptr{i}", [128, 512], F32)) for i in range(2)]

    sem_names = ["sT", "sA", "sV"] + [f"lk_{e}" for e in ENGS]
    for cls, nsem in DMA_POOLS.items():
        sem_names += [f"{cls}{k}" for k in range(nsem)]
    sems = {n: E(nc.semaphore(n, num=220 + k))
            for k, n in enumerate(sem_names)}

    Brep = arena[:, 0:DC * C]
    Crep = arena[:, DC * C:2 * DC * C]
    h1T = arena[:, 0:2 * DC * C]
    quarter = [pool[:, i * 2048:(i + 1) * 2048] for i in range(4)]

    def program(P: Prog):
        P.lk_budget = lockstep
        P.lk_skip = lockstep_skip
        def V(fn, sem=None, inc=1):
            P.op("vector", fn, sem, inc)

        def A(fn, sem=None, inc=1):
            P.op("scalar", fn, sem, inc)

        def TE(fn, sem=None, inc=1):
            P.op("tensor", fn, sem, inc)

        st = {
            "wdt_free": [0, 0],
            "tmp_free": [0, 0],
            "delta_free": [0, 0],
            "aTq_free": [0, 0],
            "ptr_free": [0, 0],
            "q_free": [0, 0, 0, 0],
            "ptwA": [0, 0, 0, 0],
            "ptwV": [0, 0, 0, 0],
            "gq": 0,
            "qrot": 0,
            "grot": 0,
            "xa_freeV": 0, "xa_freeA": 0,
            "xa_load": 0,
            "lnxT_free": 0,
            "lnxtm_free": 0,
            "xp_free": [0, 0],
            "ygT_free": 0,
            "bc_free": 0,
            "h1T_free": 0,
            "scanb": [[0] * DC, [0] * DC],
            "yg_done": 0,
            "setup_w": 0,
            "sq_free": 0,
            "xres_free": [0, 0],
            "outb_free": [None, None],
            "lnready": [0, 0],
        }

        if dbg2:
            P.dma("sync", dbgA_d[:, 0:512], Win_d[896:1024, 3584:4096], "w")
            P.dma("sync", dbgA_d[:, 512:1024], W1_d[896:1024, 3584:4096], "w")
            P.dma("sync", dbgA_d[:, 1024:1536], W2_d[3968:4096, 512:1024], "w")
            P.dma("sync", dbgA_d[:, 1536:2048], Wout_d[1920:2048, 512:1024], "w")
            P.dma("sync", dbgA_d[:, 2048:2560], x_d[3968:4096, 512:1024], "w")
        if mode == "slow2":
            for it in range(4000):
                t_ = P.dma("sync", identf[:], ident_dram.ap(), "w")
                P.wait_t("sync", t_)
            for e in ENGS:
                P.wait_all(e, P.mark_all("w"))
        for fn in (AF.Silu, AF.Exp, AF.Ln, AF.Square, AF.Abs, AF.Relu,
                   AF.Identity, AF.Copy):
            A(lambda sc, fn=fn: sc.activation(out=stats[:, 30:31],
                                              in_=stats[:, 31:32], func=fn))
        A(lambda sc: sc.nop(), sem="sA")
        if report is not None and hasattr(P, "n"):
            report["op_warm"] = P.n
        warm_mark = P.mark("sA")
        for e in ("sync", "vector", "tensor", "gpsimd"):
            P.wait(e, "sA", warm_mark)
        P.dma("sync", identf[:], ident_dram.ap(), "w")
        with nc.allow_non_contiguous_dma("small constant layout loads"):
            P.dma("sync", bx_sb[:], bx_d.rearrange("(p o) -> p o", o=1), "w")
            P.dma("sync", bin_sb[:], bin_d.rearrange("(m p) -> p m", p=128), "w")
            P.dma("sync", bdt_sb[:], bdt_d.rearrange("(m p) -> p m", p=128), "w")
            P.dma("sync", b1_sb[:], b1_d.rearrange("(m p) -> p m", p=128), "w")
            P.dma("sync", Dp_sb[:], Dp_d.rearrange("(m p) -> p m", p=128), "w")
            P.dma("sync", A_sb[:].rearrange("p (m n) -> p m n", n=NS),
                  Alog_d.rearrange("(m p) n -> p m n", p=128), "w")
        P.dma("sync", lnxtm[0:1, 0:DM], bout_d.rearrange("(o m) -> o m", o=1), "w")
        hTf = hT[:].bitcast(F32)
        P.dma("sync", hTf[:, 0:DC * 96].rearrange("p (m n) -> p m n", n=96),
              Wx_d.rearrange("(m p) n -> p m n", p=128), "w")
        P.dma("sync", hTf[0:RK, DC * 96:DC * 96 + DI], Wdt_d, "w")
        const_w = P.mark_all("w")
        if report is not None and hasattr(P, "n"):
            report["op_const"] = P.n

        P.wait_all("vector", const_w)
        V(lambda v: v.memset(onesrow[:], 1.0))
        V(lambda v: v.memset(hcar[:], 0.0))
        V(lambda v: v.memset(stats[:, 28:29], 1e-5))
        V(lambda v: v.tensor_copy(out=boutrow[:], in_=lnxtm[0:1, 0:DM]), sem="sV")
        P.wait("sync", "sV")
        tb2 = P.dma("sync", lnxtm[0:1, 0:DM],
                    b2_d.rearrange("(o m) -> o m", o=1), "w")
        P.wait_t("vector", tb2)
        V(lambda v: v.tensor_copy(out=b2row[:], in_=lnxtm[0:1, 0:DM]), sem="sV")
        P.wait_all("scalar", const_w)
        A(lambda s: s.activation(out=A_sb[:], in_=A_sb[:], func=AF.Exp), sem="sA")
        P.wait("vector", "sA")
        V(lambda v: v.tensor_single_scalar(out=A_sb[:], in_=A_sb[:], scalar=-1.0,
                                           op=ALU.mult), sem="sV")
        A(lambda s: s.activation(out=WxB[:], in_=hTf[:, 0:DC * 96], func=AF.Copy))
        A(lambda s: s.activation(out=WdtB[:], in_=hTf[0:RK, DC * 96:DC * 96 + DI],
                                 func=AF.Copy), sem="sA")
        prev_cast = [P.mark("sA"), P.mark("sA")]
        if report is not None and hasattr(P, "n"):
            report["op_wxwdt"] = P.n

        rounds = []
        for r in range(8):
            for cc in range(2):
                rounds.append((wbf_win, r, cc * 2048, 2048, Win_d))
        for r in range(8):
            for cc in range(2):
                rounds.append((wbf_w1, r, cc * 2048, 2048, W1_d))
        for r in range(16):
            rounds.append((wbf_wout, r, 0, DM, Wout_d))
        for r in range(32):
            rounds.append((wbf_w2, r, 0, DM, W2_d))
        prev_store = [None, None]
        for ri, (dst, r, c0, nc_, src) in enumerate(rounds):
            hb = ri % 2
            fview = hTf[:, hb * 2048:hb * 2048 + nc_]
            bview = szT[:, hb * 2048:hb * 2048 + nc_]
            P.wait("sync", "sA", prev_cast[hb])
            ld = P.dma("sync", fview, src[r * 128:(r + 1) * 128, c0:c0 + nc_], "w")
            P.wait_t("scalar", ld)
            P.wait_t("scalar", prev_store[hb])
            A(lambda s, fview=fview, bview=bview: s.activation(
                out=bview, in_=fview, func=AF.Copy), sem="sA")
            cm = P.mark("sA")
            P.wait("sync", "sA", cm)
            prev_store[hb] = P.dma(
                "sync", dst[r * 128:(r + 1) * 128, c0:c0 + nc_], bview, "w")
            prev_cast[hb] = cm
        st["setup_w"] = P.mark_all("w")
        if report is not None and hasattr(P, "n"):
            report["op_rounds"] = P.n

        def ln_tile(i, ld_ticket, do_tp=True):
            xai = xa[:, i * DM:(i + 1) * DM]
            if isinstance(ld_ticket, list):
                for t_ in ld_ticket:
                    P.wait_t("vector", t_)
                    P.wait_t("scalar", t_)
                ld_ticket = ld_ticket[-1]
            P.wait_t("vector", ld_ticket)
            V(lambda v, xai=xai, i=i: v.tensor_reduce(
                out=stats[:, i:i + 1], in_=xai, axis=AX.X, op=ALU.add), sem="sV")
            P.wait_t("scalar", ld_ticket)
            P.wait("scalar", "sV", st["sq_free"])
            P.wait("scalar", "sT", st["lnxtm_free"])
            A(lambda s, xai=xai: s.activation(out=lnxtm[:], in_=xai,
                                              func=AF.Square), sem="sA")
            sq_mark = P.mark("sA")
            P.wait("vector", "sA", sq_mark)
            V(lambda v, i=i: v.tensor_reduce(
                out=stats[:, 4 + i:5 + i], in_=lnxtm[:], axis=AX.X, op=ALU.add))
            V(lambda v, i=i: v.tensor_single_scalar(
                out=stats[:, 8 + i:9 + i], in_=stats[:, i:i + 1],
                scalar=1.0 / DM, op=ALU.mult))
            V(lambda v, i=i: v.tensor_mul(
                out=stats[:, 12 + i:13 + i], in0=stats[:, 8 + i:9 + i],
                in1=stats[:, 8 + i:9 + i]))
            V(lambda v, i=i: v.tensor_scalar(
                out=stats[:, 16 + i:17 + i], in0=stats[:, 4 + i:5 + i],
                scalar1=1.0 / DM, scalar2=stats[:, 12 + i:13 + i],
                op0=ALU.mult, op1=ALU.subtract), sem="sV")
            st["sq_free"] = P.mark("sV")
            P.wait("scalar", "sV", st["sq_free"])
            A(lambda s, i=i: s.activation(out=stats[:, 20 + i:21 + i],
                                          in_=stats[:, 16 + i:17 + i],
                                          func=AF.Ln, bias=stats[:, 28:29]))
            A(lambda s, i=i: s.activation(out=stats[:, 24 + i:25 + i],
                                          in_=stats[:, 20 + i:21 + i],
                                          func=AF.Exp, scale=-0.5), sem="sA")
            inv_mark = P.mark("sA")
            if not do_tp:
                return
            P.wait("vector", "sA", inv_mark)
            V(lambda v, xai=xai, i=i: v.tensor_scalar(
                out=lnxtm[:], in0=xai,
                scalar1=stats[:, 8 + i:9 + i], scalar2=stats[:, 24 + i:25 + i],
                op0=ALU.subtract, op1=ALU.mult), sem="sV")
            lnx_mark = P.mark("sV")
            st["sq_free"] = lnx_mark
            lnv = lnxT[:].rearrange("p (kc c) -> p kc c", c=C)
            for hh in range(2):
                b = (i * 2 + hh) % 2
                P.wait("tensor", "sV", lnx_mark)
                P.wait("tensor", "sV", st["ptr_free"][b])
                for j in range(4):
                    kc = hh * 4 + j
                    TE(lambda t, kc=kc, j=j, b=b: t.transpose(
                        ptr[b][:, j * 128:(j + 1) * 128],
                        lnxtm[:, kc * 128:(kc + 1) * 128], identf[:]),
                       sem=("sT" if j == 3 else None))
                if hh == 1:
                    st["lnxtm_free"] = P.mark("sT")
                tp_mark = P.mark("sT")
                P.wait("vector", "sT", tp_mark)
                if i == 0 and hh == 0:
                    P.wait("vector", "sT", st["lnxT_free"])
                V(lambda v, hh=hh, i=i, b=b, lnv=lnv: v.tensor_copy(
                    out=lnv[:, hh * 4:(hh + 1) * 4, i * 128:(i + 1) * 128],
                    in_=ptr[b][:].rearrange("p (j c) -> p j c", c=128)),
                  sem="sV")
                st["ptr_free"][b] = P.mark("sV")
            if i == MT - 1:
                st["xa_freeV"] = P.mark("sV")
                st["xa_freeA"] = P.mark("sA")

        def ln_apply(i):
            xai = xa[:, i * DM:(i + 1) * DM]
            P.wait("vector", "sT", st["lnxtm_free"])
            V(lambda v, xai=xai, i=i: v.tensor_scalar(
                out=lnxtm[:], in0=xai,
                scalar1=stats[:, 8 + i:9 + i], scalar2=stats[:, 24 + i:25 + i],
                op0=ALU.subtract, op1=ALU.mult), sem="sV")
            lnx_mark = P.mark("sV")
            st["sq_free"] = lnx_mark
            lnv = lnxT[:].rearrange("p (kc c) -> p kc c", c=C)
            for hh in range(2):
                b = (i * 2 + hh) % 2
                P.wait("tensor", "sV", lnx_mark)
                P.wait("tensor", "sV", st["ptr_free"][b])
                for j in range(4):
                    kc = hh * 4 + j
                    TE(lambda t, kc=kc, j=j, b=b: t.transpose(
                        ptr[b][:, j * 128:(j + 1) * 128],
                        lnxtm[:, kc * 128:(kc + 1) * 128], identf[:]),
                       sem=("sT" if j == 3 else None))
                if hh == 1:
                    st["lnxtm_free"] = P.mark("sT")
                tp_mark = P.mark("sT")
                P.wait("vector", "sT", tp_mark)
                if i == 0 and hh == 0:
                    P.wait("vector", "sT", st["lnxT_free"])
                V(lambda v, hh=hh, i=i, b=b, lnv=lnv: v.tensor_copy(
                    out=lnv[:, hh * 4:(hh + 1) * 4, i * 128:(i + 1) * 128],
                    in_=ptr[b][:].rearrange("p (j c) -> p j c", c=128)),
                  sem="sV")
                st["ptr_free"][b] = P.mark("sV")
            if i == MT - 1:
                st["xa_freeV"] = P.mark("sV")
                st["xa_freeA"] = P.mark("sA")

        def stream_quarter(wbf, jq, kc0=0):
            qb = st["qrot"] % 4
            st["qrot"] += 1
            P.wait("sync", "sT", st["q_free"][qb])
            for k8 in range(8):
                kc = kc0 + k8
                tk = P.dma("sync", quarter[qb][:, k8 * 256:(k8 + 1) * 256],
                           wbf[kc * 128:(kc + 1) * 128,
                               jq * 256:(jq + 1) * 256], "w")
            return qb, tk

        def front_quarter(wbf, jq, lnready, spar, evac):
            qb, slab_mark = stream_quarter(wbf, jq)
            for m in range(2):
                mi = jq * 2 + m
                bank = mi % 4
                P.wait_t("tensor", slab_mark)
                P.wait("tensor", "sA", st["ptwA"][bank])
                P.wait("tensor", "sV", st["ptwV"][bank])
                P.wait("tensor", "sV", lnready)
                for kc in range(KC):
                    TE(lambda t, kc=kc, m=m, qb=qb, bank=bank: t.matmul(
                        pt[bank][:],
                        quarter[qb][:, kc * 256 + m * 128:kc * 256 + (m + 1) * 128],
                        lnxT[:, kc * C:(kc + 1) * C],
                        start=(kc == 0), stop=(kc == KC - 1)),
                       sem=("sT" if kc == KC - 1 else None))
                mm_mark = P.mark("sT")
                P.wait("scalar", "sT", mm_mark)
                evac(mi, bank, spar)
                st["ptwA"][bank] = P.mark("sA")
            st["q_free"][qb] = P.mark("sT")

        def silu_evac(mi, bank, spar):
            if mi < DC:
                dst, di = xsT, mi
            else:
                dst, di = szT, mi - DC
            if spar is not None:
                P.wait("scalar", "sV", st["scanb"][spar][di])
            A(lambda s, dst=dst, di=di, bank=bank, mi=mi: s.activation(
                out=dst[:, di * C:(di + 1) * C], in_=pt[bank][:],
                func=AF.Silu, bias=bin_sb[:, mi:mi + 1]), sem="sA")

        def relu_evac(mi, bank, spar):
            if mi == 0:
                P.wait("scalar", "sT", st["h1T_free"])
            A(lambda s, mi=mi, bank=bank: s.activation(
                out=h1T[:, mi * C:(mi + 1) * C], in_=pt[bank][:],
                func=AF.Relu, bias=b1_sb[:, mi:mi + 1]), sem="sA")

        def xp_phase(ch):
            par = ch % 2
            P.wait("tensor", "sA")
            P.wait("tensor", "sT", st["xp_free"][par])
            P.wait("tensor", "sV", st["ptr_free"][0])
            for kc in range(DC):
                TE(lambda t, kc=kc: t.matmul(
                    ptr[0][0:96, :], WxB[:, kc * 96:(kc + 1) * 96],
                    xsT[:, kc * C:(kc + 1) * C],
                    start=(kc == 0), stop=(kc == DC - 1)),
                   sem=("sT" if kc == DC - 1 else None))
            xp_mark = P.mark("sT")
            P.wait("vector", "sT", xp_mark)
            V(lambda v, par=par: v.tensor_scalar(
                out=xpTb[par][:], in0=ptr[0][0:96, :], scalar1=bx_sb[:],
                scalar2=None, op0=ALU.add), sem="sV")
            if dbg and ch == 0:
                V(lambda v, par=par: v.tensor_copy(out=ygT[1][:, 4096:4608][0:96],
                                                   in_=xpTb[par][:]))
            xp_done = P.mark("sV")
            st["ptr_free"][0] = xp_done
            P.wait("gpsimd", "sV", xp_done)
            P.wait("gpsimd", "sV", st["bc_free"])
            twr = P.dma("gpsimd", xp_sc[ch], xpTb[par][64:96, :], "bc")
            P.wait_t("gpsimd", twr)
            tb = P.dma("gpsimd", Brep.rearrange("p (n c) -> p n c", c=C),
                       xp_sc[ch, 0:NS, :].unsqueeze(0)
                       .broadcast_to([128, NS, C]), "bc")
            tc = P.dma("gpsimd", Crep.rearrange("p (n c) -> p n c", c=C),
                       xp_sc[ch, NS:2 * NS, :].unsqueeze(0)
                       .broadcast_to([128, NS, C]), "bc")
            return xp_done, (tb, tc)

        def scan_dc(ch, dc, xp_done, bc_done):
            par = dc % 2
            cpar = ch % 2
            P.wait("tensor", "sA", st["wdt_free"][par])
            P.wait("tensor", "sV", xp_done)
            TE(lambda t, dc=dc, par=par, cpar=cpar: t.matmul(
                pt[4 + par][:], WdtB[:, dc * 128:(dc + 1) * 128],
                xpTb[cpar][0:RK, :], start=True, stop=True), sem="sT")
            wdt_mark = P.mark("sT")
            if dc == DC - 1:
                st["xp_free"][cpar] = wdt_mark
            P.wait("scalar", "sT", wdt_mark)
            P.wait("scalar", "sV", st["tmp_free"][par])
            A(lambda s, dc=dc, par=par: s.activation(
                out=tmp1[par][:], in_=pt[4 + par][:], func=AF.Abs,
                bias=bdt_sb[:, dc:dc + 1]))
            A(lambda s, par=par: s.activation(
                out=tmp2[par][:], in_=tmp1[par][:], func=AF.Exp, scale=-1.0))
            A(lambda s, par=par: s.activation(
                out=tmp1[par][:], in_=tmp2[par][:], func=AF.Ln, bias=1.0))
            A(lambda s, dc=dc, par=par: s.activation(
                out=tmp2[par][:], in_=pt[4 + par][:], func=AF.Relu,
                bias=bdt_sb[:, dc:dc + 1]), sem="sA")
            if dbg and ch == 0 and dc == 1:
                A(lambda s, cpar=cpar: s.activation(
                    out=ygT[1][:, 4608:5120][0:96], in_=xpTb[cpar][:],
                    func=AF.Copy))
                A(lambda s, par=par: s.activation(
                    out=ygT[1][:, 2048:2560], in_=tmp1[par][:], func=AF.Copy))
                A(lambda s, par=par: s.activation(
                    out=ygT[1][:, 2560:3072], in_=tmp2[par][:], func=AF.Copy))
                A(lambda s, par=par: s.activation(
                    out=ygT[1][:, 1024 + 512:1024 + 1024],
                    in_=pt[4 + par][:], func=AF.Copy))
            st["wdt_free"][par] = P.mark("sA")
            sp_mark = P.mark("sA")
            P.wait("vector", "sA", sp_mark)
            P.wait("vector", "sA", st["delta_free"][par])
            if ch == 0 and dc == 0:
                P.wait_all("vector", st["setup_w"])
            V(lambda v, par=par: v.tensor_add(
                out=delta[par][:], in0=tmp1[par][:], in1=tmp2[par][:]))
            V(lambda v, dc=dc, par=par: v.tensor_mul(
                out=dxd[:], in0=delta[par][:],
                in1=xsT[:, dc * C:(dc + 1) * C]), sem="sV")
            st["tmp_free"][par] = P.mark("sV")
            delta_mark = P.mark("sV")
            if dbg and ch == 0 and dc == 1:
                V(lambda v, par=par: v.tensor_copy(out=ygT[1][:, 0:512],
                                                   in_=delta[par][:]))
                V(lambda v: v.tensor_copy(out=ygT[1][:, 512:1024], in_=dxd[:]))

            for q in range(8):
                qb = st["gq"] % 2
                P.wait("scalar", "sV", delta_mark)
                P.wait("scalar", "sV", st["aTq_free"][qb])
                for j in range(2):
                    n = q * 2 + j
                    A(lambda s, dc=dc, n=n, j=j, par=par, qb=qb: s.activation(
                        out=aTq[qb][:, j * C:(j + 1) * C], in_=delta[par][:],
                        func=AF.Exp,
                        scale=A_sb[:, dc * NS + n:dc * NS + n + 1]),
                      sem=("sA" if j == 1 else None))
                aexp_mark = P.mark("sA")
                if dc == 0 and q == 0:
                    P.wait_t("vector", bc_done[0])
                    P.wait_t("vector", bc_done[1])
                for j in range(2):
                    n = q * 2 + j
                    V(lambda v, n=n, j=j, qb=qb: v.tensor_mul(
                        out=bTq[qb][:, j * C:(j + 1) * C], in0=dxd[:],
                        in1=Brep[:, n * C:(n + 1) * C]))
                P.wait("vector", "sA", aexp_mark)
                if dbg and ch == 0 and dc == 1 and q == 0:
                    V(lambda v, qb=qb: v.tensor_copy(out=ygT[1][:, 1024:1536],
                                                     in_=aTq[qb][:, 0:512]))
                    V(lambda v: v.tensor_copy(out=ygT[1][:, 3072:3584],
                                              in_=Brep[:, 0:512]))
                for j in range(2):
                    n = q * 2 + j
                    V(lambda v, dc=dc, n=n, j=j, qb=qb: v.tensor_tensor_scan(
                        out=hT[:, n * C:(n + 1) * C],
                        data0=aTq[qb][:, j * C:(j + 1) * C],
                        data1=bTq[qb][:, j * C:(j + 1) * C],
                        initial=hcar[:, dc * NS + n:dc * NS + n + 1],
                        op0=ALU.mult, op1=ALU.add),
                      sem=("sV" if j == 1 else None))
                st["aTq_free"][qb] = P.mark("sV")
                st["gq"] += 1
            st["delta_free"][par] = P.mark("sA")

            hv = hT[:].rearrange("p (n c) -> p n c", c=C)
            V(lambda v, dc=dc, hv=hv: v.tensor_copy(
                out=hcar[:, dc * NS:(dc + 1) * NS]
                .rearrange("p (n o) -> p n o", o=1),
                in_=hv[:, :, C - 1:C]))
            V(lambda v: v.tensor_mul(out=hT[:], in0=hT[:], in1=Crep[:]))
            V(lambda v: v.tensor_add(out=hT[:, 0:8 * C], in0=hT[:, 0:8 * C],
                                     in1=hT[:, 8 * C:16 * C]))
            V(lambda v: v.tensor_add(out=hT[:, 0:4 * C], in0=hT[:, 0:4 * C],
                                     in1=hT[:, 4 * C:8 * C]))
            V(lambda v: v.tensor_add(out=hT[:, 0:2 * C], in0=hT[:, 0:2 * C],
                                     in1=hT[:, 2 * C:4 * C]))
            V(lambda v: v.tensor_add(out=yf[:], in0=hT[:, 0:C],
                                     in1=hT[:, C:2 * C]))
            V(lambda v, dc=dc: v.scalar_tensor_tensor(
                out=yf[:], in0=xsT[:, dc * C:(dc + 1) * C],
                scalar=Dp_sb[:, dc:dc + 1], in1=yf[:],
                op0=ALU.mult, op1=ALU.add))
            if dc == 0:
                P.wait("vector", "sT", st["ygT_free"])
            V(lambda v, dc=dc, cpar=cpar: v.tensor_mul(
                out=ygT[cpar][:, dc * C:(dc + 1) * C], in0=yf[:],
                in1=szT[:, dc * C:(dc + 1) * C]), sem="sV")
            if dbg and ch == 0 and dc == 1:
                V(lambda v: v.tensor_copy(out=ygT[1][:, 3584:4096],
                                          in_=hT[:, 0:512]), sem="sV")
            if dbg and ch == 0 and dc == DC - 1:
                V(lambda v: v.tensor_copy(
                    out=ygT[1][:, 2048:2304].bitcast(F32)[:, 0:128]
                    if False else ygT[1][:, 1024:1280],
                    in_=hcar[:, 0:256]))
                yv = ygT[0][:].rearrange("p (d c) -> p d c", c=C)
                V(lambda v, yv=yv: v.tensor_copy(
                    out=ygT[1][:, 0:512].rearrange("p (d c) -> p d c", c=32),
                    in_=yv[:, :, 48:80]), sem="sV")
                dm = P.mark("sV")
                P.wait("gpsimd", "sV", dm)
                P.dma("gpsimd", dbg_d, ygT[1][:, 0:5120], "o")
            st["scanb"][cpar][dc] = P.mark("sV")
            if dc == DC - 1:
                st["yg_done"] = P.mark("sV")
                st["bc_free"] = st["yg_done"]

        def out_group(ch, gi, n_pieces, wbf, lhsT_buf, ready_sV, ready_sA,
                      resid_src, biasrow, last):
            ncol_groups = DM // 256
            mt = gi // ncol_groups
            cg = gi % ncol_groups
            col0 = cg * 256
            bank = gi % 4
            gb = st["grot"] % 2
            st["grot"] += 1
            P.wait("gpsimd", "sV", st["xres_free"][gb])
            xres_mark = P.dma("gpsimd", xres[gb][:, 0:256],
                              resid_src[ch * C + mt * 128:ch * C + (mt + 1) * 128,
                                        col0:col0 + 256], "x")
            pieces = []
            for pc in range(n_pieces):
                qb, ldm = stream_quarter(wbf, cg, pc * 8)
                pieces.append((qb, ldm))
            P.wait("tensor", "sV", ready_sV)
            P.wait("tensor", "sA", ready_sA)
            P.wait("tensor", "sA", st["ptwA"][bank])
            P.wait("tensor", "sV", st["ptwV"][bank])
            for pc, (qb, ldm) in enumerate(pieces):
                P.wait_t("tensor", ldm)
                for k8 in range(8):
                    kc = pc * 8 + k8
                    TE(lambda t, kc=kc, mt=mt, qb=qb, k8=k8, bank=bank: t.matmul(
                        pt[bank][:, 0:256],
                        lhsT_buf[:, kc * C + mt * 128:kc * C + (mt + 1) * 128],
                        quarter[qb][:, k8 * 256:(k8 + 1) * 256],
                        start=(kc == 0), stop=False))
            TE(lambda t, col0=col0, bank=bank: t.matmul(
                pt[bank][:, 0:256], onesrow[:], biasrow[:, col0:col0 + 256],
                start=False, stop=True), sem="sT")
            for qb, _ in pieces:
                st["q_free"][qb] = P.mark("sT")
            mm_mark = P.mark("sT")
            P.wait("vector", "sT", mm_mark)
            P.wait_t("vector", xres_mark)
            P.wait_t("vector", st["outb_free"][gb])
            V(lambda v, bank=bank, gb=gb: v.scalar_tensor_tensor(
                out=outb[gb][:, 0:256], in0=pt[bank][:, 0:256], scalar=1.0,
                in1=xres[gb][:, 0:256], op0=ALU.mult, op1=ALU.add), sem="sV")
            ev = P.mark("sV")
            st["xres_free"][gb] = ev
            st["ptwV"][bank] = ev
            P.wait("gpsimd", "sV", ev)
            st["outb_free"][gb] = P.dma(
                "gpsimd",
                out_d[ch * C + mt * 128:ch * C + (mt + 1) * 128, col0:col0 + 256],
                outb[gb][:, 0:256], "o")
            if last:
                if wbf is wbf_wout:
                    st["ygT_free"] = mm_mark
                else:
                    st["h1T_free"] = mm_mark

        def load_chunk(ch, src):
            P.wait("gpsimd", "sV", st["xa_freeV"])
            P.wait("gpsimd", "sA", st["xa_freeA"])
            st["xa_load"] = [
                P.dma("gpsimd", xa[:, i * DM:(i + 1) * DM],
                      src[ch * C + i * 128:ch * C + (i + 1) * 128, :], "x")
                for i in range(MT)]

        def front_serial(nxt, ch, src):
            load_chunk(nxt, src)
            for i in range(MT):
                ln_tile(i, list(st["xa_load"]))
            st["lnready"][nxt % 2] = P.mark("sV")
            for j in range(6):
                front_quarter(wbf_win, j, st["lnready"][nxt % 2],
                              ch % 2, silu_evac)
                front_quarter(wbf_win, 8 + j, st["lnready"][nxt % 2],
                              ch % 2, silu_evac)

        if mode == "slow":
            for it in range(200):
                t_ = P.dma("sync", identf[:], ident_dram.ap(), "w")
                P.wait_t("sync", t_)
        for e in ENGS:
            P.wait_all(e, st["setup_w"])
            P.wait(e, "sA")
            P.wait(e, "sV")
        P.wait("scalar", "sV")
        load_chunk(0, x_d)
        if lockstep == 0 and lockstep_skip == 0:
            P.lk_budget = 10 ** 6
            P.lk_dma = False
        for i in range(MT):
            ln_tile(i, list(st["xa_load"]))
        st["lnready"][0] = P.mark("sV")
        P.wait_all("scalar", st["setup_w"])
        for jq in range(8):
            front_quarter(wbf_win, jq, st["lnready"][0], None, silu_evac)
            front_quarter(wbf_win, 8 + jq, st["lnready"][0], None, silu_evac)
        st["lnxT_free"] = P.mark("sT")
        xp_done, bc_done = xp_phase(0)

        interleave_front = (not serial) and mode in ("full", "front", "fln", "flstats")
        if mode == "barrier":
            interleave_front = False
        interleave_win = (not serial) and mode in ("full", "front")
        interleave_wout = (not serial) and mode in ("full", "wout")
        if mode == "barrier":
            interleave_wout = False
        yg_done_by_ch = {}

        def wout_all(ch):
            for gi in range(16):
                out_group(ch, gi, 2, wbf_wout, ygT[ch % 2],
                          yg_done_by_ch[ch], 0, x_d, boutrow, last=(gi == 15))

        for ch in range(NCH):
            nxt = ch + 1 if ch + 1 < NCH else None
            for dc in range(DC):
                scan_dc(ch, dc, xp_done, bc_done)
                if interleave_front and nxt is not None:
                    if dc == 0:
                        load_chunk(nxt, x_d)
                    if 1 <= dc <= 4:
                        ln_tile(dc - 1, st["xa_load"][dc - 1],
                                do_tp=(mode != "flstats"))
                        if dc == 4 and mode != "flstats":
                            st["lnready"][nxt % 2] = P.mark("sV")
                    if interleave_win and dc >= 5 and dc % 2 == 1:
                        j = (dc - 5) // 2
                        if j < 6:
                            front_quarter(wbf_win, j, st["lnready"][nxt % 2],
                                          ch % 2, silu_evac)
                            front_quarter(wbf_win, 8 + j, st["lnready"][nxt % 2],
                                          ch % 2, silu_evac)
                if interleave_wout and ch >= 1 and 2 <= dc <= 9:
                    for g in range(2):
                        gi = (dc - 2) * 2 + g
                        out_group(ch - 1, gi, 2, wbf_wout, ygT[(ch - 1) % 2],
                                  yg_done_by_ch[ch - 1], 0, x_d, boutrow,
                                  last=(gi == 15))
            yg_done_by_ch[ch] = st["yg_done"]
            if ch == 0 and lockstep == 0 and lockstep_skip == 0:
                P.lk_budget = 0
            if not interleave_wout and ch >= 1:
                wout_all(ch - 1)
            if nxt is not None:
                if not interleave_front:
                    front_serial(nxt, ch, x_d)
                elif mode == "flstats":
                    for i in range(MT):
                        ln_apply(i)
                    st["lnready"][nxt % 2] = P.mark("sV")
                    for j in range(6):
                        front_quarter(wbf_win, j, st["lnready"][nxt % 2],
                                      ch % 2, silu_evac)
                        front_quarter(wbf_win, 8 + j, st["lnready"][nxt % 2],
                                      ch % 2, silu_evac)
                elif not interleave_win:
                    for j in range(6):
                        front_quarter(wbf_win, j, st["lnready"][nxt % 2],
                                      ch % 2, silu_evac)
                        front_quarter(wbf_win, 8 + j, st["lnready"][nxt % 2],
                                      ch % 2, silu_evac)
                for j in (6, 7):
                    front_quarter(wbf_win, j, st["lnready"][nxt % 2],
                                  ch % 2, silu_evac)
                    front_quarter(wbf_win, 8 + j, st["lnready"][nxt % 2],
                                  ch % 2, silu_evac)
                st["lnxT_free"] = P.mark("sT")
                xp_done, bc_done = xp_phase(nxt)
        wout_all(NCH - 1)

        if stop_after == "passA":
            P.wait_all("gpsimd", P.mark_all("o"))
            return

        P.wait_all("gpsimd", P.mark_all("o"))
        load_chunk(0, out_d)
        for i in range(MT):
            ln_tile(i, list(st["xa_load"]))
        st["lnready"][0] = P.mark("sV")

        for ch in range(NCH):
            nxt = ch + 1 if ch + 1 < NCH else None
            for jq in range(16):
                front_quarter(wbf_w1, jq, st["lnready"][ch % 2], None, relu_evac)
                if jq == 0 and nxt is not None:
                    load_chunk(nxt, out_d)
            st["lnxT_free"] = P.mark("sT")
            relu_done = P.mark("sA")
            for gi in range(16):
                out_group(ch, gi, 4, wbf_w2, h1T, 0, relu_done, out_d, b2row,
                          last=(gi == 15))
                if nxt is not None and gi in (0, 1, 2, 3):
                    ln_tile(gi, list(st["xa_load"]))
                    if gi == 3:
                        st["lnready"][nxt % 2] = P.mark("sV")

        P.wait_all("gpsimd", P.mark_all("o"))
        if dbg2:
            P.dma("sync", dbgB_d[:, 0:512], wbf_win[896:1024, 3584:4096], "o")
            P.dma("sync", dbgB_d[:, 512:1024], wbf_w1[896:1024, 3584:4096], "o")
            P.dma("sync", dbgB_d[:, 1024:1536], wbf_w2[3968:4096, 512:1024], "o")
            P.dma("sync", dbgB_d[:, 1536:2048], wbf_wout[1920:2048, 512:1024], "o")
            P.wait_all("gpsimd", P.mark_all("o"))
            P.wait_all("gpsimd", P.mark_all("w"))

    if recorder is not None:
        program(recorder)
        es.close()
        return None, sbuf_left

    final_ctrs = []
    with nc.Block() as block:
        for name in ENGS:
            def section(eng_obj, name=name):
                p = Prog(name, eng_obj, sems)
                program(p)
                final_ctrs.append((name, dict(p.ctr)))
            getattr(block, name)(section)
    base = final_ctrs[0][1]
    for name, c in final_ctrs[1:]:
        assert c == base, ("walk divergence", name,
                           {k: (v, base[k]) for k, v in c.items() if v != base[k]})

    es.close()
    return nc, sbuf_left


_NC_CACHE = {}


def kernel(**inputs):
    x = np.ascontiguousarray(np.asarray(inputs["L0_attn_x"], dtype=np.float32))
    names = {
        "Win": "L0_attn_Win", "bin": "L0_attn_bin", "Wx": "L0_attn_Wx",
        "bx": "L0_attn_bx", "Wdt": "L0_attn_Wdt", "bdt": "L0_attn_bdt",
        "A_log": "L0_attn_A_log", "Dp": "L0_attn_Dp", "Wout": "L0_attn_Wout",
        "bout": "L0_attn_bout", "W1": "L0_ffn_W1", "b1": "L0_ffn_b1",
        "W2": "L0_ffn_W2", "b2": "L0_ffn_b2",
    }
    weights = {k: np.ascontiguousarray(np.asarray(inputs[v], dtype=np.float32))
               for k, v in names.items()}
    if "nc" not in _NC_CACHE:
        _NC_CACHE["nc"] = build_nc()[0]
    nc = _NC_CACHE["nc"]
    in_maps = [dict(weights, x=x[c]) for c in range(8)]
    res = run_bass_kernel_spmd(nc, in_maps, core_ids=list(range(8)))
    return np.stack([np.asarray(r["out"], dtype=np.float32) for r in res.results])


# version 11 (final)
# speedup vs baseline: 1.3168x; 1.2720x over previous
import numpy as np
from contextlib import ExitStack

import concourse.bass as bass
import concourse.mybir as mybir
from concourse.bass_utils import run_bass_kernel_spmd

F32 = mybir.dt.float32
BF16 = mybir.dt.bfloat16
AF = mybir.ActivationFunctionType
ALU = mybir.AluOpType
AX = mybir.AxisListType

T = 4096
DM = 1024
DI = 2048
NS = 16
RK = 64
HI = 4096
C = 512
NCH = T // C
MT = C // 128
KC = DM // 128
DC = DI // 128
HC = HI // 128

ENGS = ("sync", "tensor", "vector", "scalar", "gpsimd")


DMA_POOLS = {"x": 8, "w": 12, "bc": 3, "o": 4}


class Prog:

    def __init__(self, emit_eng, eng_obj, sems):
        self.emit = emit_eng
        self.e = eng_obj
        self.sems = sems
        self.ctr = {k: 0 for k in sems}
        self.rot = {k: 0 for k in DMA_POOLS}
        self.lk_budget = 0
        self.lk_skip = 0
        self.lk_dma = True
        self.lk_te = True

    def op(self, eng, fn, sem=None, inc=1):
        if self.lk_skip > 0:
            self.lk_skip -= 1
            lk = False
        else:
            lk = self.lk_budget > 0 and (eng != "tensor" or self.lk_te)
        if lk:
            self.lk_budget -= 1
            if sem is None:
                sem = f"lk_{eng}"
        if eng == self.emit:
            ins = fn(self.e)
            if sem is not None:
                ins.then_inc(self.sems[sem], inc)
        if sem is not None:
            self.ctr[sem] += inc
            if lk:
                self.wait(self.emit, sem, self.ctr[sem])

    def dma(self, eng, out, in_, cls):
        k = self.rot[cls] % DMA_POOLS[cls]
        self.rot[cls] += 1
        sem = f"{cls}{k}"
        if self.ctr[sem] > 0:
            self.wait(eng, sem, self.ctr[sem])
        lk = self.lk_skip <= 0 and self.lk_budget > 0 and self.lk_dma
        b, self.lk_budget = self.lk_budget, 0
        self.op(eng, lambda e: e.dma_start(out=out, in_=in_), sem=sem, inc=16)
        self.lk_budget = (b - 1) if lk else b
        if lk:
            self.wait(self.emit, sem, self.ctr[sem])
        return (sem, self.ctr[sem])

    def wait_t(self, eng, ticket):
        if ticket is None:
            return
        sem, val = ticket
        self.wait(eng, sem, val)

    def mark_all(self, cls):
        return tuple((f"{cls}{k}", self.ctr[f"{cls}{k}"])
                     for k in range(DMA_POOLS[cls]))

    def wait_all(self, eng, marks):
        for sem, val in marks:
            self.wait(eng, sem, val)

    def wait(self, eng, sem, val=None):
        if val is None:
            val = self.ctr[sem]
        assert val <= self.ctr[sem], ("wait on future inc", sem, val, self.ctr[sem])
        if val <= 0:
            return
        if eng == self.emit:
            self.e.wait_ge(self.sems[sem], val)

    def mark(self, sem):
        return self.ctr[sem]


def build_nc(report=None, stop_after=None, serial=False, mode="full",
             recorder=None, dbg=False, dbg2=False, lockstep=0,
             lockstep_skip=0):
    nc = bass.Bass("TRN2", target_bir_lowering=False, debug=False)

    x_d = nc.dram_tensor("x", [T, DM], F32, kind="ExternalInput").ap()
    Win_d = nc.dram_tensor("Win", [DM, 2 * DI], F32, kind="ExternalInput").ap()
    bin_d = nc.dram_tensor("bin", [2 * DI], F32, kind="ExternalInput").ap()
    Wx_d = nc.dram_tensor("Wx", [DI, 96], F32, kind="ExternalInput").ap()
    bx_d = nc.dram_tensor("bx", [96], F32, kind="ExternalInput").ap()
    Wdt_d = nc.dram_tensor("Wdt", [RK, DI], F32, kind="ExternalInput").ap()
    bdt_d = nc.dram_tensor("bdt", [DI], F32, kind="ExternalInput").ap()
    Alog_d = nc.dram_tensor("A_log", [DI, NS], F32, kind="ExternalInput").ap()
    Dp_d = nc.dram_tensor("Dp", [DI], F32, kind="ExternalInput").ap()
    Wout_d = nc.dram_tensor("Wout", [DI, DM], F32, kind="ExternalInput").ap()
    bout_d = nc.dram_tensor("bout", [DM], F32, kind="ExternalInput").ap()
    W1_d = nc.dram_tensor("W1", [DM, HI], F32, kind="ExternalInput").ap()
    b1_d = nc.dram_tensor("b1", [HI], F32, kind="ExternalInput").ap()
    W2_d = nc.dram_tensor("W2", [HI, DM], F32, kind="ExternalInput").ap()
    b2_d = nc.dram_tensor("b2", [DM], F32, kind="ExternalInput").ap()
    out_d = nc.dram_tensor("out", [T, DM], F32, kind="ExternalOutput").ap()
    dbg_d = (nc.dram_tensor("dbg", [128, 5120], BF16, kind="ExternalOutput").ap()
             if dbg else None)
    dbgA_d = (nc.dram_tensor("dbgA", [128, 2560], F32, kind="ExternalOutput").ap()
              if dbg2 else None)
    dbgB_d = (nc.dram_tensor("dbgB", [128, 2048], BF16, kind="ExternalOutput").ap()
              if dbg2 else None)

    wbf_win = nc.dram_tensor("wbf_win", [DM, 2 * DI], BF16, kind="Internal").ap()
    wbf_w1 = nc.dram_tensor("wbf_w1", [DM, HI], BF16, kind="Internal").ap()
    wbf_w2 = nc.dram_tensor("wbf_w2", [HI, DM], BF16, kind="Internal").ap()
    wbf_wout = nc.dram_tensor("wbf_wout", [DI, DM], BF16, kind="Internal").ap()
    xp_sc = nc.dram_tensor("xp_sc", [NCH, 32, C], BF16, kind="Internal").ap()
    ident_dram = nc.inline_tensor(np.eye(128, dtype=np.float32), name="ident128")

    es = ExitStack()
    E = es.enter_context

    xa = E(nc.sbuf_tensor("xa", [128, MT * DM], F32))
    lnxtm = E(nc.sbuf_tensor("lnxtm", [128, DM], F32))
    stats = E(nc.sbuf_tensor("stats", [128, 32], F32))
    lnxT = E(nc.sbuf_tensor("lnxT", [128, KC * C], BF16))
    xsT = E(nc.sbuf_tensor("xsT", [128, DC * C], BF16))
    szT = E(nc.sbuf_tensor("szT", [128, DC * C], BF16))
    xpTb = [E(nc.sbuf_tensor(f"xpTb{i}", [96, C], BF16)) for i in range(2)]
    arena = E(nc.sbuf_tensor("arena", [128, 2 * DC * C], BF16))
    delta = [E(nc.sbuf_tensor(f"delta{i}", [128, C], F32)) for i in range(2)]
    tmp1 = [E(nc.sbuf_tensor(f"tmp1{i}", [128, C], F32)) for i in range(2)]
    tmp2 = [E(nc.sbuf_tensor(f"tmp2{i}", [128, C], F32)) for i in range(2)]
    aTq = [E(nc.sbuf_tensor(f"aTq{i}", [128, 2 * C], F32)) for i in range(2)]
    bTq0 = E(nc.sbuf_tensor("bTq0", [128, 2 * C], BF16))
    bTq = [bTq0, bTq0]
    hT = E(nc.sbuf_tensor("hT", [128, DC * C], BF16))
    hcar = E(nc.sbuf_tensor("hcar", [128, DC * NS], F32))
    yf = E(nc.sbuf_tensor("yf", [128, C], F32))
    dxd = E(nc.sbuf_tensor("dxd", [128, C], BF16))
    ygT = [E(nc.sbuf_tensor(f"ygT{i}", [128, DC * C], BF16)) for i in range(2)]
    pool = E(nc.sbuf_tensor("pool", [128, 8192], BF16))
    xres = [E(nc.sbuf_tensor(f"xres{i}", [128, 512], F32)) for i in range(2)]
    outb = [E(nc.sbuf_tensor(f"outb{i}", [128, 512], F32)) for i in range(2)]
    WxB = E(nc.sbuf_tensor("WxB", [128, DC * 96], BF16))
    WdtB = E(nc.sbuf_tensor("WdtB", [RK, DI], BF16))
    A_sb = E(nc.sbuf_tensor("A_sb", [128, DC * NS], F32))
    Dp_sb = E(nc.sbuf_tensor("Dp_sb", [128, DC], F32))
    bin_sb = E(nc.sbuf_tensor("bin_sb", [128, 32], F32))
    bdt_sb = E(nc.sbuf_tensor("bdt_sb", [128, DC], F32))
    b1_sb = E(nc.sbuf_tensor("b1_sb", [128, HC], F32))
    bx_sb = E(nc.sbuf_tensor("bx_sb", [96, 1], F32))
    boutrow = E(nc.sbuf_tensor("boutrow", [1, DM], BF16))
    b2row = E(nc.sbuf_tensor("b2row", [1, DM], BF16))
    onesrow = E(nc.sbuf_tensor("onesrow", [1, 128], BF16))
    identf = E(nc.sbuf_tensor("identf", [128, 128], F32))

    sbuf_left = nc.sbuf_bytes_remaining
    if report is not None:
        report["sbuf_left"] = sbuf_left

    pt = [E(nc.psum_tensor(f"pt{i}", [128, 512], F32)) for i in range(6)]
    ptr = [E(nc.psum_tensor(f"ptr{i}", [128, 512], F32)) for i in range(2)]

    sem_names = ["sT", "sA", "sV"] + [f"lk_{e}" for e in ENGS]
    for cls, nsem in DMA_POOLS.items():
        sem_names += [f"{cls}{k}" for k in range(nsem)]
    sems = {n: E(nc.semaphore(n, num=220 + k))
            for k, n in enumerate(sem_names)}

    Brep = arena[:, 0:DC * C]
    Crep = arena[:, DC * C:2 * DC * C]
    h1T = arena[:, 0:2 * DC * C]
    quarter = [pool[:, i * 2048:(i + 1) * 2048] for i in range(4)]

    def program(P: Prog):
        P.lk_budget = lockstep
        P.lk_skip = lockstep_skip
        def V(fn, sem=None, inc=1):
            P.op("vector", fn, sem, inc)

        def A(fn, sem=None, inc=1):
            P.op("scalar", fn, sem, inc)

        def TE(fn, sem=None, inc=1):
            P.op("tensor", fn, sem, inc)

        st = {
            "wdt_free": [0, 0],
            "tmp_free": [0, 0],
            "delta_free": [0, 0],
            "aTq_free": [0, 0],
            "ptr_free": [0, 0],
            "q_free": [0, 0, 0, 0],
            "ptwA": [0, 0, 0, 0],
            "ptwV": [0, 0, 0, 0],
            "gq": 0,
            "qrot": 0,
            "grot": 0,
            "xa_freeV": 0, "xa_freeA": 0,
            "xa_load": 0,
            "lnxT_free": 0,
            "lnxtm_free": 0,
            "xp_free": [0, 0],
            "ygT_free": 0,
            "bc_free": 0,
            "h1T_free": 0,
            "scanb": [[0] * DC, [0] * DC],
            "yg_done": 0,
            "setup_w": 0,
            "sq_free": 0,
            "xres_free": [0, 0],
            "outb_free": [None, None],
            "lnready": [0, 0],
        }

        if dbg2:
            P.dma("sync", dbgA_d[:, 0:512], Win_d[896:1024, 3584:4096], "w")
            P.dma("sync", dbgA_d[:, 512:1024], W1_d[896:1024, 3584:4096], "w")
            P.dma("sync", dbgA_d[:, 1024:1536], W2_d[3968:4096, 512:1024], "w")
            P.dma("sync", dbgA_d[:, 1536:2048], Wout_d[1920:2048, 512:1024], "w")
            P.dma("sync", dbgA_d[:, 2048:2560], x_d[3968:4096, 512:1024], "w")
        if mode == "slow2":
            for it in range(4000):
                t_ = P.dma("sync", identf[:], ident_dram.ap(), "w")
                P.wait_t("sync", t_)
            for e in ENGS:
                P.wait_all(e, P.mark_all("w"))
        for fn in (AF.Silu, AF.Exp, AF.Ln, AF.Square, AF.Abs, AF.Relu,
                   AF.Identity, AF.Copy):
            A(lambda sc, fn=fn: sc.activation(out=stats[:, 30:31],
                                              in_=stats[:, 31:32], func=fn))
        A(lambda sc: sc.nop(), sem="sA")
        if report is not None and hasattr(P, "n"):
            report["op_warm"] = P.n
        warm_mark = P.mark("sA")
        for e in ("sync", "vector", "tensor", "gpsimd"):
            P.wait(e, "sA", warm_mark)
        P.dma("sync", identf[:], ident_dram.ap(), "w")
        with nc.allow_non_contiguous_dma("small constant layout loads"):
            P.dma("sync", bx_sb[:], bx_d.rearrange("(p o) -> p o", o=1), "w")
            P.dma("sync", bin_sb[:], bin_d.rearrange("(m p) -> p m", p=128), "w")
            P.dma("sync", bdt_sb[:], bdt_d.rearrange("(m p) -> p m", p=128), "w")
            P.dma("sync", b1_sb[:], b1_d.rearrange("(m p) -> p m", p=128), "w")
            P.dma("sync", Dp_sb[:], Dp_d.rearrange("(m p) -> p m", p=128), "w")
            P.dma("sync", A_sb[:].rearrange("p (m n) -> p m n", n=NS),
                  Alog_d.rearrange("(m p) n -> p m n", p=128), "w")
        P.dma("sync", lnxtm[0:1, 0:DM], bout_d.rearrange("(o m) -> o m", o=1), "w")
        hTf = hT[:].bitcast(F32)
        P.dma("sync", hTf[:, 0:DC * 96].rearrange("p (m n) -> p m n", n=96),
              Wx_d.rearrange("(m p) n -> p m n", p=128), "w")
        P.dma("sync", hTf[0:RK, DC * 96:DC * 96 + DI], Wdt_d, "w")
        const_w = P.mark_all("w")
        if report is not None and hasattr(P, "n"):
            report["op_const"] = P.n

        P.wait_all("vector", const_w)
        V(lambda v: v.memset(onesrow[:], 1.0))
        V(lambda v: v.memset(hcar[:], 0.0))
        V(lambda v: v.memset(stats[:, 28:29], 1e-5))
        V(lambda v: v.tensor_copy(out=boutrow[:], in_=lnxtm[0:1, 0:DM]), sem="sV")
        P.wait("sync", "sV")
        tb2 = P.dma("sync", lnxtm[0:1, 0:DM],
                    b2_d.rearrange("(o m) -> o m", o=1), "w")
        P.wait_t("vector", tb2)
        V(lambda v: v.tensor_copy(out=b2row[:], in_=lnxtm[0:1, 0:DM]), sem="sV")
        P.wait_all("scalar", const_w)
        A(lambda s: s.activation(out=A_sb[:], in_=A_sb[:], func=AF.Exp), sem="sA")
        P.wait("vector", "sA")
        V(lambda v: v.tensor_single_scalar(out=A_sb[:], in_=A_sb[:], scalar=-1.0,
                                           op=ALU.mult), sem="sV")
        A(lambda s: s.activation(out=WxB[:], in_=hTf[:, 0:DC * 96], func=AF.Copy))
        A(lambda s: s.activation(out=WdtB[:], in_=hTf[0:RK, DC * 96:DC * 96 + DI],
                                 func=AF.Copy), sem="sA")
        prev_cast = [P.mark("sA"), P.mark("sA")]
        if report is not None and hasattr(P, "n"):
            report["op_wxwdt"] = P.n

        rounds = []
        for r in range(8):
            for cc in range(2):
                rounds.append((wbf_win, r, cc * 2048, 2048, Win_d))
        for r in range(8):
            for cc in range(2):
                rounds.append((wbf_w1, r, cc * 2048, 2048, W1_d))
        for r in range(16):
            rounds.append((wbf_wout, r, 0, DM, Wout_d))
        for r in range(32):
            rounds.append((wbf_w2, r, 0, DM, W2_d))
        prev_store = [None, None]
        for ri, (dst, r, c0, nc_, src) in enumerate(rounds):
            hb = ri % 2
            fview = hTf[:, hb * 2048:hb * 2048 + nc_]
            bview = szT[:, hb * 2048:hb * 2048 + nc_]
            P.wait("sync", "sA", prev_cast[hb])
            ld = P.dma("sync", fview, src[r * 128:(r + 1) * 128, c0:c0 + nc_], "w")
            P.wait_t("scalar", ld)
            P.wait_t("scalar", prev_store[hb])
            A(lambda s, fview=fview, bview=bview: s.activation(
                out=bview, in_=fview, func=AF.Copy), sem="sA")
            cm = P.mark("sA")
            P.wait("gpsimd", "sA", cm)
            prev_store[hb] = P.dma(
                "gpsimd", dst[r * 128:(r + 1) * 128, c0:c0 + nc_], bview, "w")
            prev_cast[hb] = cm
        st["setup_w"] = P.mark_all("w")
        if report is not None and hasattr(P, "n"):
            report["op_rounds"] = P.n

        def ln_tile(i, ld_ticket, do_tp=True):
            xai = xa[:, i * DM:(i + 1) * DM]
            if isinstance(ld_ticket, list):
                for t_ in ld_ticket:
                    P.wait_t("vector", t_)
                    P.wait_t("scalar", t_)
                ld_ticket = ld_ticket[-1]
            P.wait_t("vector", ld_ticket)
            V(lambda v, xai=xai, i=i: v.tensor_reduce(
                out=stats[:, i:i + 1], in_=xai, axis=AX.X, op=ALU.add), sem="sV")
            P.wait_t("scalar", ld_ticket)
            P.wait("scalar", "sV", st["sq_free"])
            P.wait("scalar", "sT", st["lnxtm_free"])
            A(lambda s, xai=xai: s.activation(out=lnxtm[:], in_=xai,
                                              func=AF.Square), sem="sA")
            sq_mark = P.mark("sA")
            P.wait("vector", "sA", sq_mark)
            V(lambda v, i=i: v.tensor_reduce(
                out=stats[:, 4 + i:5 + i], in_=lnxtm[:], axis=AX.X, op=ALU.add))
            V(lambda v, i=i: v.tensor_single_scalar(
                out=stats[:, 8 + i:9 + i], in_=stats[:, i:i + 1],
                scalar=1.0 / DM, op=ALU.mult))
            V(lambda v, i=i: v.tensor_mul(
                out=stats[:, 12 + i:13 + i], in0=stats[:, 8 + i:9 + i],
                in1=stats[:, 8 + i:9 + i]))
            V(lambda v, i=i: v.tensor_scalar(
                out=stats[:, 16 + i:17 + i], in0=stats[:, 4 + i:5 + i],
                scalar1=1.0 / DM, scalar2=stats[:, 12 + i:13 + i],
                op0=ALU.mult, op1=ALU.subtract), sem="sV")
            st["sq_free"] = P.mark("sV")
            P.wait("scalar", "sV", st["sq_free"])
            A(lambda s, i=i: s.activation(out=stats[:, 20 + i:21 + i],
                                          in_=stats[:, 16 + i:17 + i],
                                          func=AF.Ln, bias=stats[:, 28:29]))
            A(lambda s, i=i: s.activation(out=stats[:, 24 + i:25 + i],
                                          in_=stats[:, 20 + i:21 + i],
                                          func=AF.Exp, scale=-0.5), sem="sA")
            inv_mark = P.mark("sA")
            if not do_tp:
                return
            P.wait("vector", "sA", inv_mark)
            V(lambda v, xai=xai, i=i: v.tensor_scalar(
                out=lnxtm[:], in0=xai,
                scalar1=stats[:, 8 + i:9 + i], scalar2=stats[:, 24 + i:25 + i],
                op0=ALU.subtract, op1=ALU.mult), sem="sV")
            lnx_mark = P.mark("sV")
            st["sq_free"] = lnx_mark
            lnv = lnxT[:].rearrange("p (kc c) -> p kc c", c=C)
            for hh in range(2):
                b = (i * 2 + hh) % 2
                P.wait("tensor", "sV", lnx_mark)
                P.wait("tensor", "sV", st["ptr_free"][b])
                for j in range(4):
                    kc = hh * 4 + j
                    TE(lambda t, kc=kc, j=j, b=b: t.transpose(
                        ptr[b][:, j * 128:(j + 1) * 128],
                        lnxtm[:, kc * 128:(kc + 1) * 128], identf[:]),
                       sem=("sT" if j == 3 else None))
                if hh == 1:
                    st["lnxtm_free"] = P.mark("sT")
                tp_mark = P.mark("sT")
                P.wait("vector", "sT", tp_mark)
                if i == 0 and hh == 0:
                    P.wait("vector", "sT", st["lnxT_free"])
                V(lambda v, hh=hh, i=i, b=b, lnv=lnv: v.tensor_copy(
                    out=lnv[:, hh * 4:(hh + 1) * 4, i * 128:(i + 1) * 128],
                    in_=ptr[b][:].rearrange("p (j c) -> p j c", c=128)),
                  sem="sV")
                st["ptr_free"][b] = P.mark("sV")
            if i == MT - 1:
                st["xa_freeV"] = P.mark("sV")
                st["xa_freeA"] = P.mark("sA")

        def ln_apply(i):
            xai = xa[:, i * DM:(i + 1) * DM]
            P.wait("vector", "sT", st["lnxtm_free"])
            V(lambda v, xai=xai, i=i: v.tensor_scalar(
                out=lnxtm[:], in0=xai,
                scalar1=stats[:, 8 + i:9 + i], scalar2=stats[:, 24 + i:25 + i],
                op0=ALU.subtract, op1=ALU.mult), sem="sV")
            lnx_mark = P.mark("sV")
            st["sq_free"] = lnx_mark
            lnv = lnxT[:].rearrange("p (kc c) -> p kc c", c=C)
            for hh in range(2):
                b = (i * 2 + hh) % 2
                P.wait("tensor", "sV", lnx_mark)
                P.wait("tensor", "sV", st["ptr_free"][b])
                for j in range(4):
                    kc = hh * 4 + j
                    TE(lambda t, kc=kc, j=j, b=b: t.transpose(
                        ptr[b][:, j * 128:(j + 1) * 128],
                        lnxtm[:, kc * 128:(kc + 1) * 128], identf[:]),
                       sem=("sT" if j == 3 else None))
                if hh == 1:
                    st["lnxtm_free"] = P.mark("sT")
                tp_mark = P.mark("sT")
                P.wait("vector", "sT", tp_mark)
                if i == 0 and hh == 0:
                    P.wait("vector", "sT", st["lnxT_free"])
                V(lambda v, hh=hh, i=i, b=b, lnv=lnv: v.tensor_copy(
                    out=lnv[:, hh * 4:(hh + 1) * 4, i * 128:(i + 1) * 128],
                    in_=ptr[b][:].rearrange("p (j c) -> p j c", c=128)),
                  sem="sV")
                st["ptr_free"][b] = P.mark("sV")
            if i == MT - 1:
                st["xa_freeV"] = P.mark("sV")
                st["xa_freeA"] = P.mark("sA")

        def stream_quarter(wbf, jq, kc0=0):
            qb = st["qrot"] % 4
            st["qrot"] += 1
            P.wait("sync", "sT", st["q_free"][qb])
            tk = P.dma("sync",
                       quarter[qb][:].rearrange("p (k8 c) -> p k8 c", c=256),
                       wbf[kc0 * 128:(kc0 + 8) * 128, jq * 256:(jq + 1) * 256]
                       .rearrange("(k8 p) c -> p k8 c", p=128), "w")
            return qb, tk

        def front_quarter(wbf, jq, lnready, spar, evac):
            qb, slab_mark = stream_quarter(wbf, jq)
            for m in range(2):
                mi = jq * 2 + m
                bank = mi % 4
                P.wait_t("tensor", slab_mark)
                P.wait("tensor", "sA", st["ptwA"][bank])
                P.wait("tensor", "sV", st["ptwV"][bank])
                P.wait("tensor", "sV", lnready)
                for kc in range(KC):
                    TE(lambda t, kc=kc, m=m, qb=qb, bank=bank: t.matmul(
                        pt[bank][:],
                        quarter[qb][:, kc * 256 + m * 128:kc * 256 + (m + 1) * 128],
                        lnxT[:, kc * C:(kc + 1) * C],
                        start=(kc == 0), stop=(kc == KC - 1)),
                       sem=("sT" if kc == KC - 1 else None))
                mm_mark = P.mark("sT")
                P.wait("scalar", "sT", mm_mark)
                evac(mi, bank, spar)
                st["ptwA"][bank] = P.mark("sA")
            st["q_free"][qb] = P.mark("sT")

        def silu_evac(mi, bank, spar):
            if mi < DC:
                dst, di = xsT, mi
            else:
                dst, di = szT, mi - DC
            if spar is not None:
                P.wait("scalar", "sV", st["scanb"][spar][di])
            A(lambda s, dst=dst, di=di, bank=bank, mi=mi: s.activation(
                out=dst[:, di * C:(di + 1) * C], in_=pt[bank][:],
                func=AF.Silu, bias=bin_sb[:, mi:mi + 1]), sem="sA")

        def relu_evac(mi, bank, spar):
            if mi == 0:
                P.wait("scalar", "sT", st["h1T_free"])
            A(lambda s, mi=mi, bank=bank: s.activation(
                out=h1T[:, mi * C:(mi + 1) * C], in_=pt[bank][:],
                func=AF.Relu, bias=b1_sb[:, mi:mi + 1]), sem="sA")

        def xp_phase(ch):
            par = ch % 2
            P.wait("tensor", "sA")
            P.wait("tensor", "sT", st["xp_free"][par])
            P.wait("tensor", "sV", st["ptr_free"][0])
            for kc in range(DC):
                TE(lambda t, kc=kc: t.matmul(
                    ptr[0][0:96, :], WxB[:, kc * 96:(kc + 1) * 96],
                    xsT[:, kc * C:(kc + 1) * C],
                    start=(kc == 0), stop=(kc == DC - 1)),
                   sem=("sT" if kc == DC - 1 else None))
            xp_mark = P.mark("sT")
            P.wait("vector", "sT", xp_mark)
            V(lambda v, par=par: v.tensor_scalar(
                out=xpTb[par][:], in0=ptr[0][0:96, :], scalar1=bx_sb[:],
                scalar2=None, op0=ALU.add), sem="sV")
            if dbg and ch == 0:
                V(lambda v, par=par: v.tensor_copy(out=ygT[1][:, 4096:4608][0:96],
                                                   in_=xpTb[par][:]))
            xp_done = P.mark("sV")
            st["ptr_free"][0] = xp_done
            P.wait("gpsimd", "sV", xp_done)
            P.wait("gpsimd", "sV", st["bc_free"])
            twr = P.dma("gpsimd", xp_sc[ch], xpTb[par][64:96, :], "bc")
            P.wait_t("gpsimd", twr)
            tb = P.dma("gpsimd", Brep.rearrange("p (n c) -> p n c", c=C),
                       xp_sc[ch, 0:NS, :].unsqueeze(0)
                       .broadcast_to([128, NS, C]), "bc")
            tc = P.dma("gpsimd", Crep.rearrange("p (n c) -> p n c", c=C),
                       xp_sc[ch, NS:2 * NS, :].unsqueeze(0)
                       .broadcast_to([128, NS, C]), "bc")
            return xp_done, (tb, tc)

        def scan_dc(ch, dc, xp_done, bc_done):
            par = dc % 2
            cpar = ch % 2
            P.wait("tensor", "sA", st["wdt_free"][par])
            P.wait("tensor", "sV", xp_done)
            TE(lambda t, dc=dc, par=par, cpar=cpar: t.matmul(
                pt[4 + par][:], WdtB[:, dc * 128:(dc + 1) * 128],
                xpTb[cpar][0:RK, :], start=True, stop=True), sem="sT")
            wdt_mark = P.mark("sT")
            if dc == DC - 1:
                st["xp_free"][cpar] = wdt_mark
            P.wait("scalar", "sT", wdt_mark)
            P.wait("scalar", "sV", st["tmp_free"][par])
            A(lambda s, dc=dc, par=par: s.activation(
                out=tmp1[par][:], in_=pt[4 + par][:], func=AF.Abs,
                bias=bdt_sb[:, dc:dc + 1]))
            A(lambda s, par=par: s.activation(
                out=tmp2[par][:], in_=tmp1[par][:], func=AF.Exp, scale=-1.0))
            A(lambda s, par=par: s.activation(
                out=tmp1[par][:], in_=tmp2[par][:], func=AF.Ln, bias=1.0))
            A(lambda s, dc=dc, par=par: s.activation(
                out=tmp2[par][:], in_=pt[4 + par][:], func=AF.Relu,
                bias=bdt_sb[:, dc:dc + 1]), sem="sA")
            if dbg and ch == 0 and dc == 1:
                A(lambda s, cpar=cpar: s.activation(
                    out=ygT[1][:, 4608:5120][0:96], in_=xpTb[cpar][:],
                    func=AF.Copy))
                A(lambda s, par=par: s.activation(
                    out=ygT[1][:, 2048:2560], in_=tmp1[par][:], func=AF.Copy))
                A(lambda s, par=par: s.activation(
                    out=ygT[1][:, 2560:3072], in_=tmp2[par][:], func=AF.Copy))
                A(lambda s, par=par: s.activation(
                    out=ygT[1][:, 1024 + 512:1024 + 1024],
                    in_=pt[4 + par][:], func=AF.Copy))
            st["wdt_free"][par] = P.mark("sA")
            sp_mark = P.mark("sA")
            P.wait("vector", "sA", sp_mark)
            P.wait("vector", "sA", st["delta_free"][par])
            if ch == 0 and dc == 0:
                P.wait_all("vector", st["setup_w"])
            V(lambda v, par=par: v.tensor_add(
                out=delta[par][:], in0=tmp1[par][:], in1=tmp2[par][:]))
            V(lambda v, dc=dc, par=par: v.tensor_mul(
                out=dxd[:], in0=delta[par][:],
                in1=xsT[:, dc * C:(dc + 1) * C]), sem="sV")
            st["tmp_free"][par] = P.mark("sV")
            delta_mark = P.mark("sV")
            if dbg and ch == 0 and dc == 1:
                V(lambda v, par=par: v.tensor_copy(out=ygT[1][:, 0:512],
                                                   in_=delta[par][:]))
                V(lambda v: v.tensor_copy(out=ygT[1][:, 512:1024], in_=dxd[:]))

            for q in range(8):
                qb = st["gq"] % 2
                P.wait("scalar", "sV", delta_mark)
                P.wait("scalar", "sV", st["aTq_free"][qb])
                for j in range(2):
                    n = q * 2 + j
                    A(lambda s, dc=dc, n=n, j=j, par=par, qb=qb: s.activation(
                        out=aTq[qb][:, j * C:(j + 1) * C], in_=delta[par][:],
                        func=AF.Exp,
                        scale=A_sb[:, dc * NS + n:dc * NS + n + 1]),
                      sem=("sA" if j == 1 else None))
                aexp_mark = P.mark("sA")
                if dc == 0 and q == 0:
                    P.wait_t("vector", bc_done[0])
                    P.wait_t("vector", bc_done[1])
                for j in range(2):
                    n = q * 2 + j
                    V(lambda v, n=n, j=j, qb=qb: v.tensor_mul(
                        out=bTq[qb][:, j * C:(j + 1) * C], in0=dxd[:],
                        in1=Brep[:, n * C:(n + 1) * C]))
                P.wait("vector", "sA", aexp_mark)
                if dbg and ch == 0 and dc == 1 and q == 0:
                    V(lambda v, qb=qb: v.tensor_copy(out=ygT[1][:, 1024:1536],
                                                     in_=aTq[qb][:, 0:512]))
                    V(lambda v: v.tensor_copy(out=ygT[1][:, 3072:3584],
                                              in_=Brep[:, 0:512]))
                for j in range(2):
                    n = q * 2 + j
                    V(lambda v, dc=dc, n=n, j=j, qb=qb: v.tensor_tensor_scan(
                        out=hT[:, n * C:(n + 1) * C],
                        data0=aTq[qb][:, j * C:(j + 1) * C],
                        data1=bTq[qb][:, j * C:(j + 1) * C],
                        initial=hcar[:, dc * NS + n:dc * NS + n + 1],
                        op0=ALU.mult, op1=ALU.add),
                      sem=("sV" if j == 1 else None))
                st["aTq_free"][qb] = P.mark("sV")
                st["gq"] += 1
            st["delta_free"][par] = P.mark("sA")

            hv = hT[:].rearrange("p (n c) -> p n c", c=C)
            V(lambda v, dc=dc, hv=hv: v.tensor_copy(
                out=hcar[:, dc * NS:(dc + 1) * NS]
                .rearrange("p (n o) -> p n o", o=1),
                in_=hv[:, :, C - 1:C]))
            V(lambda v: v.tensor_mul(out=hT[:], in0=hT[:], in1=Crep[:]))
            V(lambda v: v.tensor_add(out=hT[:, 0:8 * C], in0=hT[:, 0:8 * C],
                                     in1=hT[:, 8 * C:16 * C]))
            V(lambda v: v.tensor_add(out=hT[:, 0:4 * C], in0=hT[:, 0:4 * C],
                                     in1=hT[:, 4 * C:8 * C]))
            V(lambda v: v.tensor_add(out=hT[:, 0:2 * C], in0=hT[:, 0:2 * C],
                                     in1=hT[:, 2 * C:4 * C]))
            V(lambda v: v.tensor_add(out=yf[:], in0=hT[:, 0:C],
                                     in1=hT[:, C:2 * C]))
            V(lambda v, dc=dc: v.scalar_tensor_tensor(
                out=yf[:], in0=xsT[:, dc * C:(dc + 1) * C],
                scalar=Dp_sb[:, dc:dc + 1], in1=yf[:],
                op0=ALU.mult, op1=ALU.add))
            if dc == 0:
                P.wait("vector", "sT", st["ygT_free"])
            V(lambda v, dc=dc, cpar=cpar: v.tensor_mul(
                out=ygT[cpar][:, dc * C:(dc + 1) * C], in0=yf[:],
                in1=szT[:, dc * C:(dc + 1) * C]), sem="sV")
            if dbg and ch == 0 and dc == 1:
                V(lambda v: v.tensor_copy(out=ygT[1][:, 3584:4096],
                                          in_=hT[:, 0:512]), sem="sV")
            if dbg and ch == 0 and dc == DC - 1:
                V(lambda v: v.tensor_copy(
                    out=ygT[1][:, 2048:2304].bitcast(F32)[:, 0:128]
                    if False else ygT[1][:, 1024:1280],
                    in_=hcar[:, 0:256]))
                yv = ygT[0][:].rearrange("p (d c) -> p d c", c=C)
                V(lambda v, yv=yv: v.tensor_copy(
                    out=ygT[1][:, 0:512].rearrange("p (d c) -> p d c", c=32),
                    in_=yv[:, :, 48:80]), sem="sV")
                dm = P.mark("sV")
                P.wait("gpsimd", "sV", dm)
                P.dma("gpsimd", dbg_d, ygT[1][:, 0:5120], "o")
            st["scanb"][cpar][dc] = P.mark("sV")
            if dc == DC - 1:
                st["yg_done"] = P.mark("sV")
                st["bc_free"] = st["yg_done"]

        def out_group(ch, gi, n_pieces, wbf, lhsT_buf, ready_sV, ready_sA,
                      resid_src, biasrow, last):
            ncol_groups = DM // 256
            mt = gi // ncol_groups
            cg = gi % ncol_groups
            col0 = cg * 256
            bank = gi % 4
            gb = st["grot"] % 2
            st["grot"] += 1
            P.wait("gpsimd", "sV", st["xres_free"][gb])
            xres_mark = P.dma("gpsimd", xres[gb][:, 0:256],
                              resid_src[ch * C + mt * 128:ch * C + (mt + 1) * 128,
                                        col0:col0 + 256], "x")
            pieces = []
            for pc in range(n_pieces):
                qb, ldm = stream_quarter(wbf, cg, pc * 8)
                pieces.append((qb, ldm))
            P.wait("tensor", "sV", ready_sV)
            P.wait("tensor", "sA", ready_sA)
            P.wait("tensor", "sA", st["ptwA"][bank])
            P.wait("tensor", "sV", st["ptwV"][bank])
            for pc, (qb, ldm) in enumerate(pieces):
                P.wait_t("tensor", ldm)
                for k8 in range(8):
                    kc = pc * 8 + k8
                    TE(lambda t, kc=kc, mt=mt, qb=qb, k8=k8, bank=bank: t.matmul(
                        pt[bank][:, 0:256],
                        lhsT_buf[:, kc * C + mt * 128:kc * C + (mt + 1) * 128],
                        quarter[qb][:, k8 * 256:(k8 + 1) * 256],
                        start=(kc == 0), stop=False))
            TE(lambda t, col0=col0, bank=bank: t.matmul(
                pt[bank][:, 0:256], onesrow[:], biasrow[:, col0:col0 + 256],
                start=False, stop=True), sem="sT")
            for qb, _ in pieces:
                st["q_free"][qb] = P.mark("sT")
            mm_mark = P.mark("sT")
            P.wait("vector", "sT", mm_mark)
            P.wait_t("vector", xres_mark)
            P.wait_t("vector", st["outb_free"][gb])
            V(lambda v, bank=bank, gb=gb: v.scalar_tensor_tensor(
                out=outb[gb][:, 0:256], in0=pt[bank][:, 0:256], scalar=1.0,
                in1=xres[gb][:, 0:256], op0=ALU.mult, op1=ALU.add), sem="sV")
            ev = P.mark("sV")
            st["xres_free"][gb] = ev
            st["ptwV"][bank] = ev
            P.wait("gpsimd", "sV", ev)
            st["outb_free"][gb] = P.dma(
                "gpsimd",
                out_d[ch * C + mt * 128:ch * C + (mt + 1) * 128, col0:col0 + 256],
                outb[gb][:, 0:256], "o")
            if last:
                if wbf is wbf_wout:
                    st["ygT_free"] = mm_mark
                else:
                    st["h1T_free"] = mm_mark

        def load_chunk(ch, src):
            P.wait("gpsimd", "sV", st["xa_freeV"])
            P.wait("gpsimd", "sA", st["xa_freeA"])
            st["xa_load"] = [
                P.dma("gpsimd", xa[:, i * DM:(i + 1) * DM],
                      src[ch * C + i * 128:ch * C + (i + 1) * 128, :], "x")
                for i in range(MT)]

        def front_serial(nxt, ch, src):
            load_chunk(nxt, src)
            for i in range(MT):
                ln_tile(i, list(st["xa_load"]))
            st["lnready"][nxt % 2] = P.mark("sV")
            for j in range(6):
                front_quarter(wbf_win, j, st["lnready"][nxt % 2],
                              ch % 2, silu_evac)
                front_quarter(wbf_win, 8 + j, st["lnready"][nxt % 2],
                              ch % 2, silu_evac)

        if mode == "slow":
            for it in range(200):
                t_ = P.dma("sync", identf[:], ident_dram.ap(), "w")
                P.wait_t("sync", t_)
        for e in ENGS:
            P.wait_all(e, st["setup_w"])
            P.wait(e, "sA")
            P.wait(e, "sV")
        P.wait("scalar", "sV")
        load_chunk(0, x_d)
        if lockstep == 0 and lockstep_skip == 0:
            P.lk_budget = 10 ** 6
            P.lk_dma = False
            P.lk_te = False
        for i in range(MT):
            ln_tile(i, list(st["xa_load"]))
        st["lnready"][0] = P.mark("sV")
        P.wait_all("scalar", st["setup_w"])
        for jq in range(8):
            front_quarter(wbf_win, jq, st["lnready"][0], None, silu_evac)
            front_quarter(wbf_win, 8 + jq, st["lnready"][0], None, silu_evac)
        st["lnxT_free"] = P.mark("sT")
        xp_done, bc_done = xp_phase(0)

        interleave_front = (not serial) and mode in ("full", "front", "fln", "flstats")
        if mode == "barrier":
            interleave_front = False
        interleave_win = (not serial) and mode in ("full", "front")
        interleave_wout = (not serial) and mode in ("full", "wout")
        if mode == "barrier":
            interleave_wout = False
        yg_done_by_ch = {}

        def wout_all(ch):
            for gi in range(16):
                out_group(ch, gi, 2, wbf_wout, ygT[ch % 2],
                          yg_done_by_ch[ch], 0, x_d, boutrow, last=(gi == 15))

        for ch in range(NCH):
            nxt = ch + 1 if ch + 1 < NCH else None
            for dc in range(DC):
                scan_dc(ch, dc, xp_done, bc_done)
                if interleave_front and nxt is not None:
                    if dc == 0:
                        load_chunk(nxt, x_d)
                    if 1 <= dc <= 4:
                        ln_tile(dc - 1, st["xa_load"][dc - 1],
                                do_tp=(mode != "flstats"))
                        if dc == 4 and mode != "flstats":
                            st["lnready"][nxt % 2] = P.mark("sV")
                    if interleave_win and dc >= 5 and dc % 2 == 1:
                        j = (dc - 5) // 2
                        if j < 6:
                            front_quarter(wbf_win, j, st["lnready"][nxt % 2],
                                          ch % 2, silu_evac)
                            front_quarter(wbf_win, 8 + j, st["lnready"][nxt % 2],
                                          ch % 2, silu_evac)
                if interleave_wout and ch >= 1 and 2 <= dc <= 9:
                    for g in range(2):
                        gi = (dc - 2) * 2 + g
                        out_group(ch - 1, gi, 2, wbf_wout, ygT[(ch - 1) % 2],
                                  yg_done_by_ch[ch - 1], 0, x_d, boutrow,
                                  last=(gi == 15))
            yg_done_by_ch[ch] = st["yg_done"]
            if ch == 0 and lockstep == 0 and lockstep_skip == 0:
                P.lk_budget = 0
            if not interleave_wout and ch >= 1:
                wout_all(ch - 1)
            if nxt is not None:
                if not interleave_front:
                    front_serial(nxt, ch, x_d)
                elif mode == "flstats":
                    for i in range(MT):
                        ln_apply(i)
                    st["lnready"][nxt % 2] = P.mark("sV")
                    for j in range(6):
                        front_quarter(wbf_win, j, st["lnready"][nxt % 2],
                                      ch % 2, silu_evac)
                        front_quarter(wbf_win, 8 + j, st["lnready"][nxt % 2],
                                      ch % 2, silu_evac)
                elif not interleave_win:
                    for j in range(6):
                        front_quarter(wbf_win, j, st["lnready"][nxt % 2],
                                      ch % 2, silu_evac)
                        front_quarter(wbf_win, 8 + j, st["lnready"][nxt % 2],
                                      ch % 2, silu_evac)
                for j in (6, 7):
                    front_quarter(wbf_win, j, st["lnready"][nxt % 2],
                                  ch % 2, silu_evac)
                    front_quarter(wbf_win, 8 + j, st["lnready"][nxt % 2],
                                  ch % 2, silu_evac)
                st["lnxT_free"] = P.mark("sT")
                xp_done, bc_done = xp_phase(nxt)
        wout_all(NCH - 1)

        if stop_after == "passA":
            P.wait_all("gpsimd", P.mark_all("o"))
            return

        P.wait_all("gpsimd", P.mark_all("o"))
        load_chunk(0, out_d)
        for i in range(MT):
            ln_tile(i, list(st["xa_load"]))
        st["lnready"][0] = P.mark("sV")

        for ch in range(NCH):
            nxt = ch + 1 if ch + 1 < NCH else None
            for jq in range(16):
                front_quarter(wbf_w1, jq, st["lnready"][ch % 2], None, relu_evac)
                if jq == 0 and nxt is not None:
                    load_chunk(nxt, out_d)
            st["lnxT_free"] = P.mark("sT")
            relu_done = P.mark("sA")
            for gi in range(16):
                out_group(ch, gi, 4, wbf_w2, h1T, 0, relu_done, out_d, b2row,
                          last=(gi == 15))
                if nxt is not None and gi in (0, 1, 2, 3):
                    ln_tile(gi, list(st["xa_load"]))
                    if gi == 3:
                        st["lnready"][nxt % 2] = P.mark("sV")

        P.wait_all("gpsimd", P.mark_all("o"))
        if dbg2:
            P.dma("sync", dbgB_d[:, 0:512], wbf_win[896:1024, 3584:4096], "o")
            P.dma("sync", dbgB_d[:, 512:1024], wbf_w1[896:1024, 3584:4096], "o")
            P.dma("sync", dbgB_d[:, 1024:1536], wbf_w2[3968:4096, 512:1024], "o")
            P.dma("sync", dbgB_d[:, 1536:2048], wbf_wout[1920:2048, 512:1024], "o")
            P.wait_all("gpsimd", P.mark_all("o"))
            P.wait_all("gpsimd", P.mark_all("w"))

    if recorder is not None:
        program(recorder)
        es.close()
        return None, sbuf_left

    final_ctrs = []
    with nc.Block() as block:
        for name in ENGS:
            def section(eng_obj, name=name):
                p = Prog(name, eng_obj, sems)
                program(p)
                final_ctrs.append((name, dict(p.ctr)))
            getattr(block, name)(section)
    base = final_ctrs[0][1]
    for name, c in final_ctrs[1:]:
        assert c == base, ("walk divergence", name,
                           {k: (v, base[k]) for k, v in c.items() if v != base[k]})

    es.close()
    return nc, sbuf_left


_NC_CACHE = {}


def kernel(**inputs):
    x = np.ascontiguousarray(np.asarray(inputs["L0_attn_x"], dtype=np.float32))
    names = {
        "Win": "L0_attn_Win", "bin": "L0_attn_bin", "Wx": "L0_attn_Wx",
        "bx": "L0_attn_bx", "Wdt": "L0_attn_Wdt", "bdt": "L0_attn_bdt",
        "A_log": "L0_attn_A_log", "Dp": "L0_attn_Dp", "Wout": "L0_attn_Wout",
        "bout": "L0_attn_bout", "W1": "L0_ffn_W1", "b1": "L0_ffn_b1",
        "W2": "L0_ffn_W2", "b2": "L0_ffn_b2",
    }
    weights = {k: np.ascontiguousarray(np.asarray(inputs[v], dtype=np.float32))
               for k, v in names.items()}
    if "nc" not in _NC_CACHE:
        _NC_CACHE["nc"] = build_nc()[0]
    nc = _NC_CACHE["nc"]
    in_maps = [dict(weights, x=x[c]) for c in range(8)]
    res = run_bass_kernel_spmd(nc, in_maps, core_ids=list(range(8)))
    return np.stack([np.asarray(r["out"], dtype=np.float32) for r in res.results])
